# version 07
# baseline (speedup 1.0000x reference)
import numpy as np
from contextlib import ExitStack

import concourse.bass as bass
import concourse.mybir as mybir
from concourse.bass_utils import run_bass_kernel_spmd

F32 = mybir.dt.float32
BF16 = mybir.dt.bfloat16
I32 = mybir.dt.int32
ALU = mybir.AluOpType
ACTF = mybir.ActivationFunctionType

B, L, D = 4, 4096, 2048
H, DK, DV = 4, 512, 256
KD, VD, LOW = 2048, 1024, 16
CH = 64
SEG, NSEG = 512, 8
PAIRS = SEG // 128
QF = 2 * DK
VF = 2 * DV
TOK = 2048
INTER, NLUT = 5504, 16
NIC = INTER // 128
SH_G = 256
SH_D = 256
OFF_U = SH_G * INTER
OFF_D = 2 * SH_G * INTER
SSZ = 2 * SH_G * INTER + INTER * SH_D
SCALE = 1.0 / float(np.sqrt(DK))
EPS = 1e-5

_BUILD_DEBUG = False
_DBG_SEG = 6


class _Cnt(object):
    __slots__ = ("sem", "n")

    def __init__(self, sem):
        self.sem = sem
        self.n = 0

    def tick(self, instr, k=1):
        instr.then_inc(self.sem, k)
        self.n += k
        return self.n


class _Waiter(object):
    def __init__(self):
        self.seen = {}

    def wait(self, eng, cnt, val=None):
        v = cnt.n if val is None else val
        if v <= 0:
            return
        k = (id(eng), id(cnt))
        if self.seen.get(k, -1) >= v:
            return
        eng.wait_ge(cnt.sem, v)
        self.seen[k] = v


def build_nc():
    nc = bass.Bass("TRN2", target_bir_lowering=False)

    d_x = nc.dram_tensor("x_all", [L, D], F32, kind="ExternalInput").ap()
    d_xo = nc.dram_tensor("x_own", [TOK, D], F32, kind="ExternalInput").ap()
    d_wq = nc.dram_tensor("wq", [D, QF], F32, kind="ExternalInput").ap()
    d_wk = nc.dram_tensor("wk", [D, QF], F32, kind="ExternalInput").ap()
    d_wv = nc.dram_tensor("wv", [D, VF], F32, kind="ExternalInput").ap()
    d_wg = nc.dram_tensor("wg", [D, VF], F32, kind="ExternalInput").ap()
    d_wgk1 = nc.dram_tensor("wgk1", [D, LOW], F32, kind="ExternalInput").ap()
    d_wgk2 = nc.dram_tensor("wgk2", [LOW, QF], F32, kind="ExternalInput").ap()
    d_bgk2 = nc.dram_tensor("bgk2v", [1, QF], F32, kind="ExternalInput").ap()
    d_wn2 = nc.dram_tensor("wnorm2", [1, VF], F32, kind="ExternalInput").ap()
    d_wo = nc.dram_tensor("wo", [VD, D], F32, kind="ExternalInput").ap()
    d_lut = nc.dram_tensor("lutv", [1, NLUT], F32, kind="ExternalInput").ap()
    d_msk = nc.dram_tensor("msk", [128, 4], F32, kind="ExternalInput").ap()
    d_u128 = nc.dram_tensor("u128", [128, 128], F32, kind="ExternalInput").ap()
    d_tril = nc.dram_tensor("trilm", [128, 128], F32, kind="ExternalInput").ap()
    d_gidx = nc.dram_tensor("gidxT", [SH_G, INTER], I32, kind="ExternalInput").ap()
    d_uidx = nc.dram_tensor("uidxT", [SH_G, INTER], I32, kind="ExternalInput").ap()
    d_didx = nc.dram_tensor("didxT", [INTER, SH_D], I32, kind="ExternalInput").ap()
    d_out = nc.dram_tensor("out_own", [TOK, D], F32, kind="ExternalOutput").ap()

    dbg = {}
    if _BUILD_DEBUG:
        dbg["q_fm"] = nc.dram_tensor("dbg_q_fm", [128, 8, SEG], BF16, kind="ExternalOutput").ap()
        dbg["k_fm"] = nc.dram_tensor("dbg_k_fm", [128, 8, SEG], BF16, kind="ExternalOutput").ap()
        dbg["v64"] = nc.dram_tensor("dbg_v64", [128, 4, VF], BF16, kind="ExternalOutput").ap()
        dbg["sp"] = nc.dram_tensor("dbg_sp", [128, 4, QF], F32, kind="ExternalOutput").ap()
        dbg["sgw"] = nc.dram_tensor("dbg_sgw", [128, 4, VF], BF16, kind="ExternalOutput").ap()
        dbg["og"] = nc.dram_tensor("dbg_og", [128, 32, VF], BF16, kind="ExternalOutput").ap()
        dbg["rs"] = nc.dram_tensor("dbg_rs", [2 * TOK, VF], BF16, kind="ExternalOutput").ap()
        dbg["x1"] = nc.dram_tensor("dbg_x1", [TOK, D], F32, kind="ExternalOutput").ap()
        dbg["wexp"] = nc.dram_tensor("dbg_wexp", [8, SSZ], BF16, kind="ExternalOutput").ap()
        dbg["ogfm"] = nc.dram_tensor("dbg_ogfm", [128, 8, TOK], BF16, kind="ExternalOutput").ap()
        dbg["r1"] = nc.dram_tensor("dbg_r1", [TOK, D], F32, kind="ExternalOutput").ap()
        dbg["xo01"] = nc.dram_tensor("dbg_xo01", [128, 2, D], F32, kind="ExternalOutput").ap()
        dbg["st1"] = nc.dram_tensor("dbg_st1", [128, 16, 12], F32, kind="ExternalOutput").ap()
        dbg["eb"] = nc.dram_tensor("dbg_eb", [128, 4, 8, 128], F32, kind="ExternalOutput").ap()
        dbg["ms"] = nc.dram_tensor("dbg_ms", [128, 4, 8], F32, kind="ExternalOutput").ap()

    xbd = nc.dram_tensor("xbi", [L, D], BF16)
    wqb = nc.dram_tensor("wqb", [D, QF], BF16)
    wkb = nc.dram_tensor("wkb", [D, QF], BF16)
    wvb = nc.dram_tensor("wvb", [D, VF], BF16)
    wgb = nc.dram_tensor("wgb", [D, VF], BF16)
    wlb = nc.dram_tensor("wlb", [D, LOW], BF16)
    w2b = nc.dram_tensor("w2b", [LOW, QF], BF16)
    wob = nc.dram_tensor("wob", [VD, D], BF16)
    ogb = nc.dram_tensor("ogb", [4 * TOK, VF], BF16)
    rsout = nc.dram_tensor("rsout", [2 * TOK, VF], BF16)
    wsh = nc.dram_tensor("wsh", [SSZ], BF16)
    wfull = nc.dram_tensor("wfull", [8, SSZ], BF16)
    x1f = nc.dram_tensor("x1f", [TOK, D], F32)
    x1b = nc.dram_tensor("x1b", [TOK, D], BF16)

    st = ExitStack()
    with st:
        c_sy = _Cnt(st.enter_context(nc.semaphore("c_sy")))
        c_gp = _Cnt(st.enter_context(nc.semaphore("c_gp")))
        c_pe = _Cnt(st.enter_context(nc.semaphore("c_pe")))
        c_dv = _Cnt(st.enter_context(nc.semaphore("c_dv")))
        c_ac = _Cnt(st.enter_context(nc.semaphore("c_ac")))
        c_cc = _Cnt(st.enter_context(nc.semaphore("c_cc")))
        W = _Waiter()
        free = {}

        def fwait(eng, name):
            e = free.get(name)
            if e is not None:
                W.wait(eng, e[0], e[1])

        def fset(name, cnt, val=None):
            free[name] = (cnt, cnt.n if val is None else val)

        sb_wn2 = st.enter_context(nc.sbuf_tensor("sb_wn2", [128, VF], F32))
        sb_bgk = st.enter_context(nc.sbuf_tensor("sb_bgk", [128, QF], F32))
        sb_lut = st.enter_context(nc.sbuf_tensor("sb_lut", [128, NLUT], F32))
        sb_msk = st.enter_context(nc.sbuf_tensor("sb_msk", [128, 4], F32))
        sb_u = st.enter_context(nc.sbuf_tensor("sb_u", [128, 128], F32))
        sb_tril = st.enter_context(nc.sbuf_tensor("sb_tril", [128, 128], F32))
        sb_eps = st.enter_context(nc.sbuf_tensor("sb_eps", [128, 1], F32))
        sb_scr = st.enter_context(nc.sbuf_tensor("sb_scr", [128, 8], F32))
        EW = 2816
        e_i32 = st.enter_context(nc.sbuf_tensor("e_i32", [128, EW], I32))
        e_bf = st.enter_context(nc.sbuf_tensor("e_bf", [128, EW], BF16))
        e_acc = st.enter_context(nc.sbuf_tensor("e_acc", [128, EW], BF16))
        e_tmp = st.enter_context(nc.sbuf_tensor("e_tmp", [128, EW], BF16))

        blk = st.enter_context(nc.Block())

        def p0_sync(sy):
            c_sy.tick(sy.dma_start(out=sb_wn2[:], in_=d_wn2.partition_broadcast(128)), 16)
            c_sy.tick(sy.dma_start(out=sb_bgk[:], in_=d_bgk2.partition_broadcast(128)), 16)
            c_sy.tick(sy.dma_start(out=sb_lut[:], in_=d_lut.partition_broadcast(128)), 16)
            c_sy.tick(sy.dma_start(out=sb_msk[:], in_=d_msk), 16)
            c_sy.tick(sy.dma_start(out=sb_u[:], in_=d_u128), 16)
            c_sy.tick(sy.dma_start(out=sb_tril[:], in_=d_tril), 16)
        blk.sync(p0_sync)

        tok_cast_m = {}

        def p0_gp(gp):
            for dst, src in ((wvb, d_wv), (wgb, d_wg), (wlb, d_wgk1),
                             (w2b, d_wgk2)):
                c_gp.tick(gp.dma_start(out=dst[:, :], in_=src), 16)
            tok_cast_m["vg"] = c_gp.n
            c_gp.tick(gp.dma_start(out=xbd[0:SEG, :], in_=d_x[0:SEG, :]), 16)
            tok_cast_m["x0"] = c_gp.n
            for dst, src in ((wqb, d_wq), (wkb, d_wk)):
                c_gp.tick(gp.dma_start(out=dst[:, :], in_=src), 16)
            tok_cast_m["qk"] = c_gp.n
            for s in range(1, NSEG):
                c_gp.tick(gp.dma_start(out=xbd[s * SEG:(s + 1) * SEG, :],
                                       in_=d_x[s * SEG:(s + 1) * SEG, :]), 16)
                tok_cast_m["x%d" % s] = c_gp.n
            c_gp.tick(gp.dma_start(out=wob[:, :], in_=d_wo), 16)
        blk.gpsimd(p0_gp)
        tok_casts = c_gp.n

        def p0_dve(v):
            v.memset(sb_eps[:], EPS)
            c_dv.tick(v.engine_nop())
        blk.vector(p0_dve)
        tok_eps = c_dv.n

        strips = []
        for dsrc, off0 in ((d_gidx, 0), (d_uidx, OFF_U)):
            for pt in range(2):
                for hf in range(2):
                    base = off0 + pt * 128 * INTER
                    coff = hf * 2752
                    strips.append((
                        dsrc[pt * 128:(pt + 1) * 128, coff:coff + 2752],
                        wsh.ap()[base:base + 128 * INTER].rearrange(
                            "(p n) -> p n", p=128)[:, coff:coff + 2752],
                        2752, None))
        ib0 = 0
        for kblk in (11, 11, 11, 10):
            base = OFF_D + ib0 * 128 * SH_D
            strips.append((
                d_didx[ib0 * 128:(ib0 + kblk) * 128, :].rearrange(
                    "(a p) n -> p a n", p=128),
                wsh.ap()[base:base + kblk * 128 * SH_D].rearrange(
                    "(a p n) -> p a n", p=128, n=SH_D),
                kblk * SH_D, kblk))
            ib0 += kblk
        n_strips = len(strips)

        def emit_strip(si):
            src, dst, width, kblk = strips[si]

            def s_sync(sy):
                fwait(sy, "e_i32")
                io = e_i32[:, 0:width]
                if kblk is not None:
                    io = io.rearrange("p (a n) -> p a n", n=SH_D)
                c_sy.tick(sy.dma_start(out=io, in_=src), 16)
            blk.sync(s_sync)
            tok_ld = c_sy.n

            def s_dve(v):
                W.wait(v, c_sy, tok_ld)
                W.wait(v, c_dv, tok_eps)
                fwait(v, "e_acc")
                ins = v.tensor_copy(e_bf[:, 0:width], e_i32[:, 0:width])
                c_dv.tick(ins)
                fset("e_i32", c_dv)
                v.tensor_copy(sb_scr[:, 0:1], sb_lut[:, 0:1])
                v.tensor_scalar(e_acc[:, 0:width], e_bf[:, 0:width], 0.0,
                                sb_lut[:, 0:1], ALU.is_equal, ALU.mult)
                for j in range(1, NLUT):
                    v.tensor_scalar(e_tmp[:, 0:width], e_bf[:, 0:width], float(j),
                                    sb_lut[:, j:j + 1], ALU.is_equal, ALU.mult)
                    v.tensor_copy(sb_scr[:, 0:1], sb_lut[:, 0:1])
                    ins = v.tensor_add(e_acc[:, 0:width], e_acc[:, 0:width],
                                       e_tmp[:, 0:width])
                c_dv.tick(ins)
            blk.vector(s_dve)
            tok_exp_dve = c_dv.n

            def s_gp(gp):
                W.wait(gp, c_dv, tok_exp_dve)
                io = e_acc[:, 0:width]
                if kblk is not None:
                    io = io.rearrange("p (a n) -> p a n", n=SH_D)
                c_gp.tick(gp.dma_start(out=dst, in_=io), 16)
                fset("e_acc", c_gp)
            blk.gpsimd(s_gp)

        def emit_ag():
            wsh_done = c_gp.n

            def g(gp):
                W.wait(gp, c_gp, wsh_done)
                c_cc.tick(gp.collective_compute(
                    "AllGather", ALU.bypass,
                    replica_groups=[list(range(8))],
                    ins=[wsh.ap().opt()],
                    outs=[wfull.ap().opt()]))
            blk.gpsimd(g)

            if _BUILD_DEBUG:
                ag_done = c_cc.n

                def gdump(gp, ag_done=ag_done):
                    W.wait(gp, c_cc, ag_done)
                    c_gp.tick(gp.dma_start(out=dbg["wexp"], in_=wfull[:, :]), 16)
                blk.gpsimd(gdump)

        emit_strip(0)
        emit_strip(1)
        strip_next = 2

        att = ExitStack()
        x_fm = att.enter_context(nc.sbuf_tensor("x_fm", [128, 16, SEG], BF16))
        wqt = att.enter_context(nc.sbuf_tensor("wqt", [128, 4, 16, 128], BF16))
        wv_sb = att.enter_context(nc.sbuf_tensor("wv_sb", [128, 16, VF], BF16))
        wg_sb = att.enter_context(nc.sbuf_tensor("wg_sb", [128, 16, VF], BF16))
        wl_sb = att.enter_context(nc.sbuf_tensor("wl_sb", [128, 16, LOW], BF16))
        w2_sb = att.enter_context(nc.sbuf_tensor("w2_sb", [LOW, QF], BF16))
        q_fm = att.enter_context(nc.sbuf_tensor("q_fm", [128, 8, SEG], BF16))
        k_fm = att.enter_context(nc.sbuf_tensor("k_fm", [128, 8, SEG], BF16))
        v128 = att.enter_context(nc.sbuf_tensor("v128", [128, 4, VF], BF16))
        sp_tok = att.enter_context(nc.sbuf_tensor("sp_tok", [128, 4, QF], F32))
        gk_t = att.enter_context(nc.sbuf_tensor("gk_t", [128, QF], F32))
        gk_s = att.enter_context(nc.sbuf_tensor("gk_s", [128, QF], F32))
        sgw = att.enter_context(nc.sbuf_tensor("sgw", [128, 4, VF], BF16))
        low_fm = att.enter_context(nc.sbuf_tensor("low_fm", [16, SEG], BF16))
        eb = att.enter_context(nc.sbuf_tensor("eb", [128, 8, 128], F32))
        emb = att.enter_context(nc.sbuf_tensor("emb", [128, 8, 128], F32))
        qe = att.enter_context(nc.sbuf_tensor("qe", [128, 2, 4, 128], BF16))
        kdt = att.enter_context(nc.sbuf_tensor("kdt", [128, 2, 4, 128], BF16))
        kdT = att.enter_context(nc.sbuf_tensor("kdT", [128, 2, 4, 128], BF16))
        abf = att.enter_context(nc.sbuf_tensor("abf", [128, 128], BF16))
        s32 = att.enter_context(nc.sbuf_tensor("s32", [128, 2, 4, DV], F32))
        sbf = att.enter_context(nc.sbuf_tensor("sbf", [128, 2, 4, DV], BF16))
        ebl = att.enter_context(nc.sbuf_tensor("ebl", [128, 2, 4], F32))
        ms_t = att.enter_context(nc.sbuf_tensor("ms_t", [128, 8], F32))
        sqs = att.enter_context(nc.sbuf_tensor("sqs", [128, DV], BF16))
        og_sg = att.enter_context(nc.sbuf_tensor("og_sg", [128, 4, VF], BF16))
        og_ms = att.enter_context(nc.sbuf_tensor("og_ms", [128, 2, 2, VF], BF16))

        ps_a = att.enter_context(nc.psum_tensor("ps_a", [128, 2, 512], F32))
        ps_bg = att.enter_context(nc.psum_tensor("ps_bg", [128, 8, 128], F32))
        ps_A = att.enter_context(nc.psum_tensor("ps_A", [128, 128], F32))
        ps_o = att.enter_context(nc.psum_tensor("ps_o", [128, DV], F32))
        ps_p2 = att.enter_context(nc.psum_tensor("ps_p2", [128, 4, DV], F32))

        def a_init(v):
            v.memset(s32[:], 0.0)
            v.memset(ebl[:], 1.0)
            c_dv.tick(v.engine_nop())
        blk.vector(a_init)

        for seg in range(NSEG):
            t0 = seg * SEG
            half = seg // 4

            def l_sync(sy):
                W.wait(sy, c_gp, tok_cast_m["vg"])
                fwait(sy, "wslab")
                for dst, srcb in ((wv_sb, wvb), (wg_sb, wgb), (wl_sb, wlb)):
                    c_sy.tick(sy.dma_start(
                        out=dst[:],
                        in_=srcb[:, :].rearrange("(c p) n -> p c n", p=128)), 16)
                c_sy.tick(sy.dma_start(out=w2_sb[:], in_=w2b[:, :]), 16)
                W.wait(sy, c_gp, tok_cast_m["x%d" % seg])
                fwait(sy, "x_fm")
                for t4 in range(4):
                    for dc in range(16):
                        c_sy.tick(sy.dma_start(
                            out=x_fm[:, dc, t4 * 128:(t4 + 1) * 128],
                            in_=xbd[t0 + t4 * 128:t0 + (t4 + 1) * 128,
                                    dc * 128:(dc + 1) * 128],
                            transpose=True), 16)
            blk.sync(l_sync)
            tok_loads = c_sy.n

            pe_toks = {}
            rot = [0]

            qk_list = []
            for qc_ in range(8):
                qk_list.append(("q", wqb, qc_))
                qk_list.append(("k", wkb, qc_))
            tok_ws = []

            def emit_qk_load(gi):
                kind, srcb, qc = qk_list[gi]
                wb = gi % 4

                def qk_load(sy, srcb=srcb, qc=qc, wb=wb):
                    W.wait(sy, c_gp, tok_cast_m["qk"])
                    fwait(sy, "wqt%d" % wb)
                    c_sy.tick(sy.dma_start(
                        out=wqt[:, wb],
                        in_=srcb[:, :].rearrange("(c p) n -> p c n", p=128)[
                            :, :, qc * 128:(qc + 1) * 128]), 16)
                blk.sync(qk_load)
                tok_ws.append(c_sy.n)

            def emit_qk(gi):
                kind, srcb, qc = qk_list[gi]
                wb = gi % 4
                b_ = rot[0] % 2
                rot[0] += 1

                def qk_pe(pe, gi=gi, wb=wb, b_=b_):
                    W.wait(pe, c_sy, tok_ws[min(gi + 1, 15)])
                    W.wait(pe, c_sy, tok_loads)
                    fwait(pe, "ps_a%d" % b_)
                    for dc in range(16):
                        ins = pe.matmul(ps_a[:, b_, :],
                                        wqt[:, wb, dc, :],
                                        x_fm[:, dc, :],
                                        start=(dc == 0), stop=(dc == 15))
                    c_pe.tick(ins)
                    fset("wqt%d" % wb, c_pe)
                blk.tensor(qk_pe)
                tok_mm = c_pe.n

                if kind == "q":
                    def ev_q(v, qc=qc, b_=b_, tok_mm=tok_mm):
                        W.wait(v, c_pe, tok_mm)
                        if _BUILD_DEBUG and qc == 0:
                            fwait(v, "pj_dump")
                        ins = v.tensor_copy(q_fm[:, qc, :], ps_a[:, b_, :])
                        c_dv.tick(ins)
                        fset("ps_a%d" % b_, c_dv)
                    blk.vector(ev_q)
                else:
                    def ev_k(sc, qc=qc, b_=b_, tok_mm=tok_mm):
                        W.wait(sc, c_pe, tok_mm)
                        if qc == 0:
                            fwait(sc, "k_fm")
                            if _BUILD_DEBUG:
                                fwait(sc, "pj_dump")
                        ins = sc.copy(k_fm[:, qc, :], ps_a[:, b_, :])
                        c_ac.tick(ins)
                        fset("ps_a%d" % b_, c_ac)
                    blk.scalar(ev_k)

            def emit_vg(kind, wsl, tc):
                b_ = rot[0] % 2
                rot[0] += 1

                def vg_pe(pe, wsl=wsl, tc=tc, b_=b_):
                    W.wait(pe, c_sy, tok_loads)
                    fwait(pe, "ps_a%d" % b_)
                    for dc in range(16):
                        ins = pe.matmul(ps_a[:, b_, :],
                                        x_fm[:, dc, tc * 128:(tc + 1) * 128],
                                        wsl[:, dc, :],
                                        start=(dc == 0), stop=(dc == 15))
                    c_pe.tick(ins)
                blk.tensor(vg_pe)
                tok_mm = c_pe.n

                if kind == "v":
                    def ev_v(v, tc=tc, b_=b_, tok_mm=tok_mm):
                        W.wait(v, c_pe, tok_mm)
                        if tc == 0:
                            fwait(v, "v128pe")
                            if _BUILD_DEBUG:
                                fwait(v, "pj_dump")
                        ins = v.tensor_copy(v128[:, tc, :], ps_a[:, b_, :])
                        c_dv.tick(ins)
                        fset("ps_a%d" % b_, c_dv)
                    blk.vector(ev_v)
                else:
                    def ev_g(sc, tc=tc, b_=b_, tok_mm=tok_mm):
                        W.wait(sc, c_pe, tok_mm)
                        fwait(sc, "sgwf%d" % tc)
                        if _BUILD_DEBUG and tc == 0:
                            fwait(sc, "pj_dump")
                        ins = sc.activation(sgw[:, tc, :], ps_a[:, b_, :],
                                            ACTF.Silu)
                        c_ac.tick(ins)
                        fset("ps_a%d" % b_, c_ac)
                    blk.scalar(ev_g)
                    sg_tok = c_ac.n

                    def sgw_mul(v, tc=tc, sg_tok=sg_tok):
                        W.wait(v, c_ac, sg_tok)
                        ins = v.tensor_mul(sgw[:, tc, :], sgw[:, tc, :],
                                           sb_wn2[:])
                        c_dv.tick(ins)
                    blk.vector(sgw_mul)

            emit_qk_load(0)
            emit_qk_load(1)
            for gi in range(16):
                if gi + 2 < 16:
                    emit_qk_load(gi + 2)
                emit_qk(gi)
            for tc in range(4):
                emit_vg("v", wv_sb, tc)
                emit_vg("g", wg_sb, tc)

            b_low = rot[0] % 2
            rot[0] += 1

            def low_pe(pe, b_=b_low):
                W.wait(pe, c_sy, tok_loads)
                fwait(pe, "ps_a%d" % b_)
                for dc in range(16):
                    ins = pe.matmul(ps_a[0:16, b_, :],
                                    wl_sb[:, dc, :],
                                    x_fm[:, dc, :],
                                    start=(dc == 0), stop=(dc == 15))
                c_pe.tick(ins)
            blk.tensor(low_pe)
            tok_lowmm = c_pe.n
            fset("wslab", c_pe)
            fset("x_fm", c_pe)

            def low_ev(v, b_=b_low, tok_lowmm=tok_lowmm):
                W.wait(v, c_pe, tok_lowmm)
                fwait(v, "low_fm")
                ins = v.tensor_copy(low_fm[:], ps_a[0:16, b_, :])
                c_dv.tick(ins)
                fset("ps_a%d" % b_, c_dv)
                pe_toks["lowev"] = c_dv.n
            blk.vector(low_ev)

            for tc in range(4):
                def gk_pe(pe, tc=tc):
                    W.wait(pe, c_dv, pe_toks["lowev"])
                    fwait(pe, "ps_bg")
                    ps_gk = ps_bg[:].rearrange("p a n -> p (a n)")
                    pe.matmul(ps_gk[:, 0:512],
                              low_fm[:, tc * 128:(tc + 1) * 128],
                              w2_sb[:, 0:512], start=True, stop=True)
                    ins = pe.matmul(ps_gk[:, 512:1024],
                                    low_fm[:, tc * 128:(tc + 1) * 128],
                                    w2_sb[:, 512:1024], start=True, stop=True)
                    c_pe.tick(ins)
                blk.tensor(gk_pe)
                tok_gkmm = c_pe.n

                def gk_dve(v, tok_gkmm=tok_gkmm):
                    W.wait(v, c_pe, tok_gkmm)
                    fwait(v, "gk_t")
                    ins = v.tensor_add(gk_t[:],
                                       ps_bg[:].rearrange("p a n -> p (a n)"),
                                       sb_bgk[:])
                    c_dv.tick(ins)
                    fset("ps_bg", c_dv)
                blk.vector(gk_dve)
                tok_gka = c_dv.n

                def gk_act(sc, tc=tc, tok_gka=tok_gka):
                    W.wait(sc, c_dv, tok_gka)
                    fwait(sc, "sp_tok")
                    if _BUILD_DEBUG and tc == 0:
                        fwait(sc, "pj_dump")
                    sc.activation(gk_s[:], gk_t[:], ACTF.Sigmoid)
                    sc.copy(sb_scr[:, 1:2], sb_eps[:])
                    ins = sc.activation(sp_tok[:, tc, :], gk_s[:], ACTF.Ln)
                    tk = c_ac.tick(ins)
                    fset("gk_t", c_ac, tk)
                blk.scalar(gk_act)
            tok_sp = c_ac.n
            free.pop("low_fm", None)
            fset("low_fm", c_pe)

            if _BUILD_DEBUG and seg == _DBG_SEG:
                def dbg_gp(gp):
                    gp.wait_ge(c_ac.sem, c_ac.n)
                    gp.wait_ge(c_dv.sem, c_dv.n)
                    c_gp.tick(gp.dma_start(out=dbg["q_fm"], in_=q_fm[:]), 16)
                    c_gp.tick(gp.dma_start(out=dbg["k_fm"], in_=k_fm[:]), 16)
                    c_gp.tick(gp.dma_start(out=dbg["v64"], in_=v128[:]), 16)
                    c_gp.tick(gp.dma_start(out=dbg["sp"], in_=sp_tok[:]), 16)
                    c_gp.tick(gp.dma_start(out=dbg["sgw"], in_=sgw[:]), 16)
                    fset("pj_dump", c_gp)
                blk.gpsimd(dbg_gp)

            for j in range(PAIRS):
                def b_pe(pe, j=j):
                    W.wait(pe, c_ac, tok_sp)
                    fwait(pe, "ps_bg")
                    for fc in range(8):
                        ins = pe.matmul(ps_bg[:, fc, :],
                                        sp_tok[:, j, fc * 128:(fc + 1) * 128],
                                        sb_u[:], start=True, stop=True)
                    c_pe.tick(ins)
                blk.tensor(b_pe)
                tok_bmm = c_pe.n

                def b_act(sc, tok_bmm=tok_bmm):
                    W.wait(sc, c_pe, tok_bmm)
                    fwait(sc, "eb")
                    if _BUILD_DEBUG:
                        fwait(sc, "ebd")
                    sc.activation(eb[:].rearrange("p a n -> p (a n)"),
                                  ps_bg[:].rearrange("p a n -> p (a n)"), ACTF.Exp)
                    ins = sc.activation(emb[:].rearrange("p a n -> p (a n)"),
                                        ps_bg[:].rearrange("p a n -> p (a n)"),
                                        ACTF.Exp, scale=-1.0)
                    c_ac.tick(ins)
                    fset("ps_bg", c_ac)
                blk.scalar(b_act)
                tok_expo = c_ac.n

                def qk_dve(v, j=j, tok_expo=tok_expo):
                    W.wait(v, c_ac, tok_expo)
                    fwait(v, "qekd")
                    for h_ in range(2):
                        v.scalar_tensor_tensor(
                            qe[:, h_, :, :],
                            q_fm[:, 4 * h_:4 * h_ + 4, j * 128:(j + 1) * 128],
                            SCALE, eb[:, 4 * h_:4 * h_ + 4, :], ALU.mult, ALU.mult)
                        v.tensor_mul(
                            kdt[:, h_, :, :],
                            k_fm[:, 4 * h_:4 * h_ + 4, j * 128:(j + 1) * 128],
                            emb[:, 4 * h_:4 * h_ + 4, :])
                    c_dv.tick(v.engine_nop())
                blk.vector(qk_dve)
                tok_qekd = c_dv.n

                if _BUILD_DEBUG and seg == _DBG_SEG:
                    def dbg_eb_gp(gp, j=j, tok_qekd=tok_qekd):
                        W.wait(gp, c_dv, tok_qekd)
                        c_gp.tick(gp.dma_start(out=dbg["eb"][:, j, :, :],
                                               in_=eb[:]), 16)
                        fset("ebd", c_gp)
                    blk.gpsimd(dbg_eb_gp)

                def kdT_sync(sy, tok_qekd=tok_qekd):
                    W.wait(sy, c_dv, tok_qekd)
                    fwait(sy, "kdT")
                    fwait(sy, "kdT_dv")
                    for h_ in range(2):
                        for dc in range(4):
                            c_sy.tick(sy.dma_start(
                                out=kdT[:, h_, dc, :],
                                in_=kdt[:, h_, dc, :], transpose=True), 16)
                blk.sync(kdT_sync)
                tok_kdT = c_sy.n

                fset("kdT_dv", c_sy, tok_kdT)

                for h_ in range(2):
                    def sbf_dve(v, h_=h_):
                        fwait(v, "sbf%d" % h_)
                        for dc in range(4):
                            ins = v.tensor_scalar(
                                sbf[:, h_, dc, :], s32[:, h_, dc, :],
                                ebl[:, h_, dc:dc + 1], None, ALU.mult)
                        c_dv.tick(ins)
                    blk.vector(sbf_dve)
                    tok_sbf = c_dv.n

                    def A_pe(pe, h_=h_, tok_qekd=tok_qekd):
                        W.wait(pe, c_dv, tok_qekd)
                        fwait(pe, "ps_A")
                        for dc in range(4):
                            ins = pe.matmul(ps_A[:, :],
                                            kdt[:, h_, dc, :],
                                            qe[:, h_, dc, :],
                                            start=(dc == 0), stop=(dc == 3))
                        c_pe.tick(ins)
                    blk.tensor(A_pe)
                    tok_A = c_pe.n

                    def Am_dve(v, tok_A=tok_A):
                        W.wait(v, c_pe, tok_A)
                        fwait(v, "abf")
                        ins = v.tensor_mul(abf[:, :], ps_A[:, :], sb_tril[:])
                        c_dv.tick(ins)
                        fset("ps_A", c_dv)
                    blk.vector(Am_dve)
                    tok_Am = c_dv.n

                    def o_pe(pe, h_=h_, j=j, tok_sbf=tok_sbf, tok_Am=tok_Am):
                        W.wait(pe, c_dv, tok_sbf)
                        fwait(pe, "ps_o")
                        for dc in range(4):
                            pe.matmul(ps_o[:, :],
                                      qe[:, h_, dc, :],
                                      sbf[:, h_, dc, :],
                                      start=(dc == 0), stop=False)
                        W.wait(pe, c_dv, tok_Am)
                        ins = pe.matmul(ps_o[:, :], abf[:, :],
                                        v128[:, j, h_ * DV:(h_ + 1) * DV],
                                        start=False, stop=True)
                        c_pe.tick(ins)
                    blk.tensor(o_pe)
                    tok_o = c_pe.n
                    fset("abf", c_pe, tok_o)
                    fset("sbf%d" % h_, c_pe, tok_o)

                    mcol = 4 * h_

                    def rms_act(sc, mcol=mcol, tok_o=tok_o):
                        W.wait(sc, c_pe, tok_o)
                        fwait(sc, "ms%d" % mcol)
                        if _BUILD_DEBUG:
                            fwait(sc, "msd")
                        sc.activation(sqs[:], ps_o[:, :], ACTF.Square,
                                      accum_out=ms_t[:, mcol:mcol + 1])
                        sc.copy(ms_t[:, 3:4], sb_eps[:])
                        sc.activation(ms_t[:, mcol + 1:mcol + 2],
                                      ms_t[:, mcol:mcol + 1], ACTF.Ln,
                                      scale=1.0 / DV, bias=sb_eps[:])
                        sc.copy(ms_t[:, 3:4], sb_eps[:])
                        ins = sc.activation(ms_t[:, mcol + 2:mcol + 3],
                                            ms_t[:, mcol + 1:mcol + 2],
                                            ACTF.Exp, scale=-0.5)
                        c_ac.tick(ins)
                    blk.scalar(rms_act)
                    tok_rstd = c_ac.n

                    def og_dve(v, h_=h_, j=j, mcol=mcol, tok_rstd=tok_rstd):
                        W.wait(v, c_ac, tok_rstd)
                        if _BUILD_DEBUG:
                            fwait(v, "ogsg%d" % j)
                        ins = v.scalar_tensor_tensor(
                            og_sg[:, j, h_ * DV:(h_ + 1) * DV],
                            ps_o[:, :], ms_t[:, mcol + 2:mcol + 3],
                            sgw[:, j, h_ * DV:(h_ + 1) * DV],
                            ALU.mult, ALU.mult)
                        c_dv.tick(ins)
                        fset("ps_o", c_dv)
                        fset("ms%d" % mcol, c_dv)
                        if h_ == 1:
                            fset("sgwf%d" % j, c_dv)
                    blk.vector(og_dve)

                    def p2_pe(pe, h_=h_, j=j, tok_kdT=tok_kdT):
                        W.wait(pe, c_sy, tok_kdT)
                        fwait(pe, "ps_p2")
                        for dc in range(4):
                            ins = pe.matmul(
                                ps_p2[:, dc, :], kdT[:, h_, dc, :],
                                v128[:, j, h_ * DV:(h_ + 1) * DV],
                                start=True, stop=True)
                        c_pe.tick(ins)
                    blk.tensor(p2_pe)
                    tok_p2 = c_pe.n

                    def s_dve(v, h_=h_, tok_p2=tok_p2):
                        W.wait(v, c_pe, tok_p2)
                        for dc in range(4):
                            v.scalar_tensor_tensor(
                                s32[:, h_, dc, :], s32[:, h_, dc, :],
                                ebl[:, h_, dc:dc + 1], ps_p2[:, dc, :],
                                ALU.mult, ALU.add)
                        v.tensor_copy(
                            ebl[:, h_, :],
                            eb[:, 4 * h_:4 * h_ + 4, 127:128]
                            .rearrange("p a n -> p (a n)"))
                        c_dv.tick(v.engine_nop())
                        fset("ps_p2", c_dv)
                    blk.vector(s_dve)
                fset("eb", c_dv)
                fset("qekd", c_pe)
                fset("kdT", c_pe)

                if _BUILD_DEBUG and seg == _DBG_SEG:
                    ms_tok = c_dv.n

                    def dbg_ms_gp(gp, j=j, ms_tok=ms_tok):
                        W.wait(gp, c_dv, ms_tok)
                        c_gp.tick(gp.dma_start(out=dbg["ms"][:, j, :],
                                               in_=ms_t[:]), 16)
                        fset("msd", c_gp)
                    blk.gpsimd(dbg_ms_gp)

                ttl = (seg % 4) * 4 + j

                def ogm_dve(v, j=j, half=half):
                    fwait(v, "og_ms%d" % (j % 2))
                    for bk in range(2):
                        ins = v.tensor_scalar(
                            og_ms[:, j % 2, bk, :],
                            og_sg[:, j, :],
                            sb_msk[:, 2 * half + bk:2 * half + bk + 1],
                            None, ALU.mult)
                    c_dv.tick(ins)
                blk.vector(ogm_dve)
                tok_ogm = c_dv.n

                def ogm_sc(sc, j=j, half=half, ttl=ttl, tok_ogm=tok_ogm):
                    W.wait(sc, c_dv, tok_ogm)
                    for bk in range(2):
                        base = (2 * half + bk) * TOK + ttl * 128
                        c_ac.tick(sc.dma_start(
                            out=ogb[base:base + 128, :],
                            in_=og_ms[:, j % 2, bk, :]), 16)
                    fset("og_ms%d" % (j % 2), c_ac)
                blk.scalar(ogm_sc)

                if _BUILD_DEBUG:
                    ttg = seg * 4 + j

                    def dbg_og_gp(gp, j=j, ttg=ttg, tok_ogm=tok_ogm):
                        W.wait(gp, c_dv, tok_ogm)
                        c_gp.tick(gp.dma_start(
                            out=dbg["og"][:, ttg, :],
                            in_=og_sg[:, j, :]), 16)
                        fset("ogsg%d" % j, c_gp)
                    blk.gpsimd(dbg_og_gp)
            fset("k_fm", c_dv)
            fset("sp_tok", c_pe)
            fset("v128pe", c_pe)

            while strip_next < n_strips and strip_next <= 2 + 2 * seg:
                emit_strip(strip_next)
                strip_next += 1
            if strip_next == n_strips:
                emit_ag()
                strip_next += 1
        if strip_next <= n_strips:
            while strip_next < n_strips:
                emit_strip(strip_next)
                strip_next += 1
            emit_ag()
        tok_ag = c_cc.n
        tok_ogb_all = c_ac.n

        def rs_gp(gp):
            W.wait(gp, c_ac, tok_ogb_all)
            c_cc.tick(gp.collective_compute(
                "ReduceScatter", ALU.add,
                replica_groups=[[0, 1], [2, 3], [4, 5], [6, 7]],
                ins=[ogb.ap().opt()],
                outs=[rsout.ap().opt()]))
        blk.gpsimd(rs_gp)
        tok_rs = c_cc.n

        att.close()

        wo_ph = ExitStack()
        rs_sb = wo_ph.enter_context(nc.sbuf_tensor("rs_sb", [128, 32, VF], BF16))
        og_fm = wo_ph.enter_context(nc.sbuf_tensor("og_fm", [128, 8, TOK], BF16))
        wo_sb = wo_ph.enter_context(nc.sbuf_tensor("wo_sb", [128, 8, D], BF16))
        xo_st = wo_ph.enter_context(nc.sbuf_tensor("xo_st", [128, 2, D], F32))
        r1_t = wo_ph.enter_context(nc.sbuf_tensor("r1_t", [128, 2, D], F32))
        dmp1 = wo_ph.enter_context(nc.sbuf_tensor("dmp1", [128, 2, D], BF16))
        st1 = wo_ph.enter_context(nc.sbuf_tensor("st1", [128, 2, 12], F32))
        x1_st = wo_ph.enter_context(nc.sbuf_tensor("x1_st", [128, 2, D], F32))
        x1b_st = wo_ph.enter_context(nc.sbuf_tensor("x1b_st", [128, 2, D], BF16))
        ps_w = wo_ph.enter_context(nc.psum_tensor("ps_w", [128, 2, D], F32))

        def wo_load(gp):
            W.wait(gp, c_cc, tok_rs)
            c_gp.tick(gp.dma_start(
                out=rs_sb[:],
                in_=rsout[:, :].rearrange("(a p) n -> p a n", p=128)), 16)
        blk.gpsimd(wo_load)
        tok_rsld = c_gp.n

        if _BUILD_DEBUG:
            def dbg_rs_gp(gp):
                W.wait(gp, c_cc, tok_rs)
                c_gp.tick(gp.dma_start(out=dbg["rs"], in_=rsout[:, :]), 16)
            blk.gpsimd(dbg_rs_gp)

        def wo_sync(sy):
            W.wait(sy, c_gp, tok_rsld)
            c_sy.tick(sy.dma_start(
                out=wo_sb[:],
                in_=wob[:, :].rearrange("(c p) n -> p c n", p=128)), 16)
            for blk_i in range(2):
                for tt in range(16):
                    for vc in range(4):
                        c_sy.tick(sy.dma_start(
                            out=og_fm[:, blk_i * 4 + vc, tt * 128:(tt + 1) * 128],
                            in_=rs_sb[:, blk_i * 16 + tt, vc * 128:(vc + 1) * 128],
                            transpose=True), 16)
            for buf in range(2):
                c_sy.tick(sy.dma_start(out=xo_st[:, buf, :],
                                       in_=d_xo[buf * 128:(buf + 1) * 128, :]), 16)
        blk.sync(wo_sync)
        tok_ogfm = c_sy.n
        free["xo0"] = (c_sy, tok_ogfm)
        free["xo1"] = (c_sy, tok_ogfm)

        if _BUILD_DEBUG:
            def dbg_ogfm_gp(gp):
                gp.wait_ge(c_sy.sem, tok_ogfm)
                c_gp.tick(gp.dma_start(out=dbg["ogfm"], in_=og_fm[:]), 16)
                c_gp.tick(gp.dma_start(out=dbg["xo01"], in_=xo_st[:]), 16)
                fset("xo_dump", c_gp)
            blk.gpsimd(dbg_ogfm_gp)

        x1_done_toks = []
        for tc in range(16):
            buf = tc % 2

            def wo_pe(pe, tc=tc, buf=buf):
                W.wait(pe, c_sy, tok_ogfm)
                fwait(pe, "ps_w%d" % buf)
                for vdc in range(8):
                    for ns in range(4):
                        ins = pe.matmul(
                            ps_w[:, buf, ns * 512:(ns + 1) * 512],
                            og_fm[:, vdc, tc * 128:(tc + 1) * 128],
                            wo_sb[:, vdc, ns * 512:(ns + 1) * 512],
                            start=(vdc == 0), stop=(vdc == 7))
                c_pe.tick(ins)
            blk.tensor(wo_pe)
            tok_wo = c_pe.n

            def ln1_dve(v, buf=buf, tok_wo=tok_wo):
                W.wait(v, c_pe, tok_wo)
                fwait(v, "xo%d" % buf)
                fwait(v, "r1a%d" % buf)
                if _BUILD_DEBUG:
                    fwait(v, "r1dmp%d" % buf)
                ins = v.tensor_add(r1_t[:, buf, :], ps_w[:, buf, :],
                                   xo_st[:, buf, :])
                c_dv.tick(ins)
                fset("ps_w%d" % buf, c_dv)
            blk.vector(ln1_dve)
            tok_r1 = c_dv.n

            if _BUILD_DEBUG:
                def dbg_r1_gp(gp, tc=tc, buf=buf, tok_r1=tok_r1):
                    W.wait(gp, c_dv, tok_r1)
                    c_gp.tick(gp.dma_start(
                        out=dbg["r1"][tc * 128:(tc + 1) * 128, :],
                        in_=r1_t[:, buf, :]), 16)
                    fset("r1dmp%d" % buf, c_gp)
                blk.gpsimd(dbg_r1_gp)

            def xo_next(sy, tc=tc, buf=buf, tok_r1=tok_r1):
                if tc + 2 < 16:
                    W.wait(sy, c_dv, tok_r1)
                    if _BUILD_DEBUG:
                        fwait(sy, "xo_dump")
                    c_sy.tick(sy.dma_start(
                        out=xo_st[:, buf, :],
                        in_=d_xo[(tc + 2) * 128:(tc + 3) * 128, :]), 16)
                    fset("xo%d" % buf, c_sy)
            blk.sync(xo_next)

            def ln1_act(sc, buf=buf, tok_r1=tok_r1):
                W.wait(sc, c_dv, tok_r1)
                if _BUILD_DEBUG:
                    fwait(sc, "st1dmp%d" % buf)
                sc.activation(dmp1[:, 0, :], r1_t[:, buf, :], ACTF.Identity,
                              accum_out=st1[:, buf, 0:1])
                sc.activation(dmp1[:, 1, :], r1_t[:, buf, :], ACTF.Square,
                              accum_out=st1[:, buf, 1:2])
                sc.copy(st1[:, buf, 2:3], st1[:, buf, 0:1])
                ins = sc.copy(st1[:, buf, 3:4], st1[:, buf, 1:2])
                c_ac.tick(ins)
            blk.scalar(ln1_act)
            tok_st1 = c_ac.n
            fset("r1a%d" % buf, c_ac, tok_st1)

            def ln1b_dve(v, buf=buf, tok_st1=tok_st1):
                W.wait(v, c_ac, tok_st1)
                v.tensor_scalar(st1[:, buf, 5:6], st1[:, buf, 3:4], 1.0 / D,
                                None, ALU.mult)
                v.tensor_mul(st1[:, buf, 6:7], st1[:, buf, 2:3],
                             st1[:, buf, 2:3])
                v.tensor_scalar(st1[:, buf, 4:5], st1[:, buf, 2:3], 1.0 / D,
                                None, ALU.mult)
                ins = v.tensor_scalar(st1[:, buf, 7:8], st1[:, buf, 6:7],
                                      -1.0 / (float(D) * float(D)),
                                      st1[:, buf, 5:6],
                                      ALU.mult, ALU.add)
                c_dv.tick(ins)
            blk.vector(ln1b_dve)
            tok_var = c_dv.n

            def ln1c_act(sc, buf=buf, tok_var=tok_var):
                W.wait(sc, c_dv, tok_var)
                sc.activation(st1[:, buf, 8:9], st1[:, buf, 7:8], ACTF.Ln,
                              bias=sb_eps[:])
                sc.copy(st1[:, buf, 10:11], sb_eps[:])
                ins = sc.activation(st1[:, buf, 9:10], st1[:, buf, 8:9],
                                    ACTF.Exp, scale=-0.5)
                c_ac.tick(ins)
            blk.scalar(ln1c_act)
            tok_rstd1 = c_ac.n

            def ln1d_dve(v, buf=buf, tok_rstd1=tok_rstd1):
                W.wait(v, c_ac, tok_rstd1)
                fwait(v, "x1s%d" % buf)
                v.tensor_scalar(x1_st[:, buf, :], r1_t[:, buf, :],
                                st1[:, buf, 4:5], st1[:, buf, 9:10],
                                ALU.subtract, ALU.mult)
                v.tensor_copy(sb_scr[:, 0:1], sb_eps[:])
                ins = v.tensor_copy(x1b_st[:, buf, :], x1_st[:, buf, :])
                c_dv.tick(ins)
            blk.vector(ln1d_dve)
            tok_x1 = c_dv.n

            def x1_sync(sy, tc=tc, buf=buf, tok_x1=tok_x1):
                W.wait(sy, c_dv, tok_x1)
                c_sy.tick(sy.dma_start(out=x1f[tc * 128:(tc + 1) * 128, :],
                                       in_=x1_st[:, buf, :]), 16)
                c_sy.tick(sy.dma_start(out=x1b[tc * 128:(tc + 1) * 128, :],
                                       in_=x1b_st[:, buf, :]), 16)
                fset("x1s%d" % buf, c_sy)
            blk.sync(x1_sync)
            x1_done_toks.append(c_sy.n)

            if _BUILD_DEBUG:
                def dbg_st1_gp(gp, tc=tc, buf=buf, tok_x1=tok_x1):
                    W.wait(gp, c_dv, tok_x1)
                    c_gp.tick(gp.dma_start(
                        out=dbg["st1"][:, tc, :], in_=st1[:, buf, :]), 16)
                    fset("st1dmp%d" % buf, c_gp)
                blk.gpsimd(dbg_st1_gp)
        tok_x1_all = c_sy.n

        if _BUILD_DEBUG:
            def dbg_x1_gp(gp):
                gp.wait_ge(c_sy.sem, tok_x1_all)
                c_gp.tick(gp.dma_start(out=dbg["x1"], in_=x1f[:, :]), 16)
            blk.gpsimd(dbg_x1_gp)

        wo_ph.close()

        ffn = ExitStack()
        x1fm = ffn.enter_context(nc.sbuf_tensor("x1fm", [128, 16, 512], BF16))
        h_fm = ffn.enter_context(nc.sbuf_tensor("h_fm", [128, NIC, 512], BF16))
        wgu = ffn.enter_context(nc.sbuf_tensor("wgu", [128, 4, 2, 16, 128], BF16))
        wd_sb = ffn.enter_context(nc.sbuf_tensor("wd_sb", [128, 4, 512], BF16))
        x1t = ffn.enter_context(nc.sbuf_tensor("x1t", [128, 2, 4, 512], F32))
        r2 = ffn.enter_context(nc.sbuf_tensor("r2", [128, 4, D], F32))
        sg2 = ffn.enter_context(nc.sbuf_tensor("sg2", [128, 2, 512], BF16))
        dmp2 = ffn.enter_context(nc.sbuf_tensor("dmp2", [128, 2, D], BF16))
        st2 = ffn.enter_context(nc.sbuf_tensor("st2", [128, 2, 12], F32))
        ostg = ffn.enter_context(nc.sbuf_tensor("ostg", [128, 2, D], F32))
        ps_g = ffn.enter_context(nc.psum_tensor("ps_g", [128, 2, 512], F32))
        ps_u = ffn.enter_context(nc.psum_tensor("ps_u", [128, 2, 512], F32))
        ps_d = ffn.enter_context(nc.psum_tensor("ps_d", [128, 4, 512], F32))

        for tt4 in range(4):
            r0 = tt4 * 512

            def f_sync(sy, tt4=tt4, r0=r0):
                W.wait(sy, c_sy, x1_done_toks[4 * tt4 + 3])
                fwait(sy, "x1fm")
                for t4 in range(4):
                    for dc in range(16):
                        c_sy.tick(sy.dma_start(
                            out=x1fm[:, dc, t4 * 128:(t4 + 1) * 128],
                            in_=x1b[r0 + t4 * 128:r0 + (t4 + 1) * 128,
                                    dc * 128:(dc + 1) * 128],
                            transpose=True), 16)
            blk.sync(f_sync)
            tok_x1fm = c_sy.n

            h_toks = {}
            gu_tok_ws = []

            def emit_gu_load(oc):
                buf = oc % 4

                def gu_load(sy, oc=oc, buf=buf):
                    W.wait(sy, c_cc, tok_ag)
                    fwait(sy, "wgu%d" % buf)
                    for gi, off0 in ((0, 0), (1, OFF_U)):
                        for jj in range(2):
                            srcj = wfull[:, off0 + jj * 128 * INTER:
                                         off0 + (jj + 1) * 128 * INTER].rearrange(
                                "c (p n) -> p c n", p=128)[
                                :, :, oc * 128:(oc + 1) * 128]
                            c_sy.tick(sy.dma_start(
                                out=wgu[:, buf, gi, jj::2, :],
                                in_=srcj), 16)
                blk.sync(gu_load)
                gu_tok_ws.append(c_sy.n)

            emit_gu_load(0)
            emit_gu_load(1)
            for oc in range(NIC):
                wb4 = oc % 4
                buf = oc % 2
                if oc + 2 < NIC:
                    emit_gu_load(oc + 2)

                def gu_pe(pe, oc=oc, wb4=wb4, buf=buf):
                    W.wait(pe, c_sy, gu_tok_ws[min(oc + 1, NIC - 1)])
                    W.wait(pe, c_sy, tok_x1fm)
                    fwait(pe, "ps_g%d" % buf)
                    for dc in range(16):
                        ins = pe.matmul(ps_g[:, buf, :],
                                        wgu[:, wb4, 0, dc, :],
                                        x1fm[:, dc, :],
                                        start=(dc == 0), stop=(dc == 15))
                    c_pe.tick(ins)
                    fwait(pe, "ps_u%d" % buf)
                    for dc in range(16):
                        ins = pe.matmul(ps_u[:, buf, :],
                                        wgu[:, wb4, 1, dc, :],
                                        x1fm[:, dc, :],
                                        start=(dc == 0), stop=(dc == 15))
                    c_pe.tick(ins)
                blk.tensor(gu_pe)
                tok_gu = c_pe.n
                fset("wgu%d" % wb4, c_pe, tok_gu)

                def silu_act(sc, buf=buf, tok_gu=tok_gu):
                    W.wait(sc, c_pe, tok_gu - 1)
                    fwait(sc, "sg2%d" % buf)
                    ins = sc.activation(sg2[:, buf, :], ps_g[:, buf, :], ACTF.Silu)
                    c_ac.tick(ins)
                    fset("ps_g%d" % buf, c_ac)
                blk.scalar(silu_act)
                tok_silu = c_ac.n

                def h_dve(v, oc=oc, buf=buf, tok_silu=tok_silu, tok_gu=tok_gu):
                    W.wait(v, c_ac, tok_silu)
                    W.wait(v, c_pe, tok_gu)
                    ins = v.tensor_mul(h_fm[:, oc, :], sg2[:, buf, :],
                                       ps_u[:, buf, :])
                    c_dv.tick(ins)
                    fset("ps_u%d" % buf, c_dv)
                    fset("sg2%d" % buf, c_dv)
                blk.vector(h_dve)
                h_toks[oc] = c_dv.n
            fset("x1fm", c_pe)

            dn_items = [(mq_, ic_) for mq_ in range(4) for ic_ in range(NIC)]
            wd_tok_ws = []

            def emit_wd_load(di):
                mq, ic = dn_items[di]
                wb = di % 4

                def wd_load(sy, mq=mq, ic=ic, wb=wb):
                    fwait(sy, "wd%d" % wb)
                    src = wfull[2 * mq:2 * mq + 2,
                                OFF_D + ic * 128 * SH_D:
                                OFF_D + (ic + 1) * 128 * SH_D].rearrange(
                        "s (p n) -> p s n", p=128)
                    c_sy.tick(sy.dma_start(
                        out=wd_sb[:, wb, :].rearrange(
                            "p (s n) -> p s n", s=2), in_=src), 16)
                blk.sync(wd_load)
                wd_tok_ws.append(c_sy.n)

            x1t_toks = []

            def emit_x1t_load(mq):
                xb = mq % 2

                def x1t_load(sy, mq=mq, xb=xb, r0=r0):
                    fwait(sy, "x1t%d" % xb)
                    c_sy.tick(sy.dma_start(
                        out=x1t[:, xb],
                        in_=x1f[r0:r0 + 512,
                                mq * 512:(mq + 1) * 512].rearrange(
                            "(a p) n -> p a n", p=128)), 16)
                blk.sync(x1t_load)
                x1t_toks.append(c_sy.n)

            emit_wd_load(0)
            emit_wd_load(1)
            emit_x1t_load(0)
            emit_x1t_load(1)
            for mq in range(4):
                for ic in range(NIC):
                    di = mq * NIC + ic
                    wb = di % 4
                    if di + 2 < len(dn_items):
                        emit_wd_load(di + 2)

                    def dn_pe(pe, mq=mq, ic=ic, wb=wb, di=di):
                        W.wait(pe, c_sy,
                               wd_tok_ws[min(di + 1, len(dn_items) - 1)])
                        W.wait(pe, c_dv, h_toks[ic])
                        if ic == 0:
                            for tc in range(4):
                                fwait(pe, "ps_d%d" % tc)
                        for tc in range(4):
                            ins = pe.matmul(ps_d[:, tc, :],
                                            h_fm[:, ic, tc * 128:(tc + 1) * 128],
                                            wd_sb[:, wb, :],
                                            start=(ic == 0), stop=(ic == NIC - 1))
                        c_pe.tick(ins)
                        fset("wd%d" % wb, c_pe)
                    blk.tensor(dn_pe)
                tok_dn = c_pe.n

                def dn_evac(v, mq=mq, tok_dn=tok_dn):
                    W.wait(v, c_pe, tok_dn)
                    W.wait(v, c_sy, x1t_toks[mq])
                    xb = mq % 2
                    for tc in range(4):
                        ins = v.tensor_add(r2[:, tc, mq * 512:(mq + 1) * 512],
                                           ps_d[:, tc, :],
                                           x1t[:, xb, tc, :])
                        c_dv.tick(ins)
                        fset("ps_d%d" % tc, c_dv)
                    fset("x1t%d" % xb, c_dv)
                blk.vector(dn_evac)
                if mq + 2 < 4:
                    emit_x1t_load(mq + 2)
            tok_r2 = c_dv.n

            for tc in range(4):
                buf = tc % 2

                def ln2_act(sc, tc=tc, buf=buf, tok_r2=tok_r2):
                    W.wait(sc, c_dv, tok_r2)
                    fwait(sc, "r2a%d" % buf)
                    sc.activation(dmp2[:, 0, :], r2[:, tc, :], ACTF.Identity,
                                  accum_out=st2[:, buf, 0:1])
                    sc.activation(dmp2[:, 1, :], r2[:, tc, :], ACTF.Square,
                                  accum_out=st2[:, buf, 1:2])
                    sc.copy(st2[:, buf, 2:3], st2[:, buf, 0:1])
                    ins = sc.copy(st2[:, buf, 3:4], st2[:, buf, 1:2])
                    c_ac.tick(ins)
                blk.scalar(ln2_act)
                tok_s2 = c_ac.n

                def ln2_dve(v, buf=buf, tok_s2=tok_s2):
                    W.wait(v, c_ac, tok_s2)
                    v.tensor_scalar(st2[:, buf, 5:6], st2[:, buf, 3:4], 1.0 / D,
                                    None, ALU.mult)
                    v.tensor_mul(st2[:, buf, 6:7], st2[:, buf, 2:3],
                                 st2[:, buf, 2:3])
                    v.tensor_scalar(st2[:, buf, 4:5], st2[:, buf, 2:3], 1.0 / D,
                                    None, ALU.mult)
                    ins = v.tensor_scalar(st2[:, buf, 7:8], st2[:, buf, 6:7],
                                          -1.0 / (float(D) * float(D)),
                                          st2[:, buf, 5:6],
                                          ALU.mult, ALU.add)
                    c_dv.tick(ins)
                blk.vector(ln2_dve)
                tok_v2 = c_dv.n

                def ln2b_act(sc, buf=buf, tok_v2=tok_v2):
                    W.wait(sc, c_dv, tok_v2)
                    sc.activation(st2[:, buf, 8:9], st2[:, buf, 7:8], ACTF.Ln,
                                  bias=sb_eps[:])
                    sc.copy(st2[:, buf, 10:11], sb_eps[:])
                    ins = sc.activation(st2[:, buf, 9:10], st2[:, buf, 8:9],
                                        ACTF.Exp, scale=-0.5)
                    c_ac.tick(ins)
                blk.scalar(ln2b_act)
                tok_rstd2 = c_ac.n

                def out_dve(v, tc=tc, buf=buf, tok_rstd2=tok_rstd2):
                    W.wait(v, c_ac, tok_rstd2)
                    fwait(v, "ostg%d" % buf)
                    ins = v.tensor_scalar(ostg[:, buf, :], r2[:, tc, :],
                                          st2[:, buf, 4:5], st2[:, buf, 9:10],
                                          ALU.subtract, ALU.mult)
                    tk = c_dv.tick(ins)
                    fset("r2a%d" % buf, c_dv, tk)
                blk.vector(out_dve)
                tok_outv = c_dv.n

                def out_sync(sy, tt4=tt4, tc=tc, buf=buf, tok_outv=tok_outv):
                    W.wait(sy, c_dv, tok_outv)
                    rr = tt4 * 512 + tc * 128
                    c_sy.tick(sy.dma_start(out=d_out[rr:rr + 128, :],
                                           in_=ostg[:, buf, :]), 16)
                    fset("ostg%d" % buf, c_sy)
                blk.sync(out_sync)
        ffn.close()
        tok_final = c_sy.n

        def fin_sync(sy):
            sy.wait_ge(c_sy.sem, tok_final)
        blk.sync(fin_sync)

        def fin_gp(gp):
            gp.wait_ge(c_sy.sem, tok_final)
            if _BUILD_DEBUG:
                gp.wait_ge(c_gp.sem, c_gp.n)
        blk.gpsimd(fin_gp)

        build_nc._counters = {"c_sy": c_sy.n, "c_gp": c_gp.n, "c_pe": c_pe.n,
                              "c_dv": c_dv.n, "c_ac": c_ac.n, "c_cc": c_cc.n}

    return nc


def _host_inputs(inputs):
    x = np.ascontiguousarray(np.asarray(inputs["L0_attn_x"], dtype=np.float32))
    Wq = np.asarray(inputs["L0_attn_Wq"], dtype=np.float32)
    Wk = np.asarray(inputs["L0_attn_Wk"], dtype=np.float32)
    Wv = np.asarray(inputs["L0_attn_Wv"], dtype=np.float32)
    Wgk1 = np.ascontiguousarray(np.asarray(inputs["L0_attn_Wgk1"], dtype=np.float32))
    Wgk2 = np.asarray(inputs["L0_attn_Wgk2"], dtype=np.float32)
    bgk2 = np.asarray(inputs["L0_attn_bgk2"], dtype=np.float32)
    Wg = np.asarray(inputs["L0_attn_Wg"], dtype=np.float32)
    Wnorm = np.asarray(inputs["L0_attn_Wnorm"], dtype=np.float32)
    Wo = np.asarray(inputs["L0_attn_Wo"], dtype=np.float32)
    lut = np.asarray(inputs["L0_ffn_lut"], dtype=np.float32)
    gidx = np.asarray(inputs["L0_ffn_gate_idx"], dtype=np.int32)
    uidx = np.asarray(inputs["L0_ffn_up_idx"], dtype=np.int32)
    didx = np.asarray(inputs["L0_ffn_down_idx"], dtype=np.int32)

    u128 = np.ascontiguousarray(np.triu(np.ones((128, 128), np.float32))
                                * (1.0 / 16.0))
    trilm = np.ascontiguousarray(np.triu(np.ones((128, 128), np.float32)))

    maps = []
    for c in range(8):
        b, p = c // 2, c % 2
        pb = 1 - p
        m = {}
        m["x_all"] = np.ascontiguousarray(x[b])
        m["x_own"] = np.ascontiguousarray(x[b, p * TOK:(p + 1) * TOK])
        m["wq"] = np.ascontiguousarray(Wq[:, p * QF:(p + 1) * QF])
        m["wk"] = np.ascontiguousarray(Wk[:, p * QF:(p + 1) * QF])
        m["wv"] = np.ascontiguousarray(Wv[:, p * VF:(p + 1) * VF])
        m["wg"] = np.ascontiguousarray(Wg[:, p * VF:(p + 1) * VF])
        m["wgk1"] = Wgk1
        m["wgk2"] = np.ascontiguousarray(Wgk2[:, p * QF:(p + 1) * QF])
        m["bgk2v"] = np.ascontiguousarray(bgk2[p * QF:(p + 1) * QF][None, :])
        m["wnorm2"] = np.ascontiguousarray(np.tile(Wnorm, 2)[None, :])
        m["wo"] = np.ascontiguousarray(np.concatenate(
            [Wo[pb * VF:(pb + 1) * VF], Wo[p * VF:(p + 1) * VF]], axis=0))
        m["lutv"] = np.ascontiguousarray(lut[None, :])
        mv = [0.0, 1.0, 1.0, 0.0] if p == 0 else [1.0, 0.0, 0.0, 1.0]
        m["msk"] = np.ascontiguousarray(
            np.broadcast_to(np.asarray(mv, np.float32)[None, :], (128, 4)))
        m["u128"] = u128
        m["trilm"] = trilm
        m["gidxT"] = np.ascontiguousarray(gidx[:, c * SH_G:(c + 1) * SH_G].T)
        m["uidxT"] = np.ascontiguousarray(uidx[:, c * SH_G:(c + 1) * SH_G].T)
        m["didxT"] = np.ascontiguousarray(didx[c * SH_D:(c + 1) * SH_D, :].T)
        maps.append(m)
    return maps


def kernel(**inputs):
    maps = _host_inputs(inputs)
    nc = build_nc()
    res = run_bass_kernel_spmd(nc, maps, core_ids=list(range(8)))
    out = np.empty((B, L, D), np.float32)
    for c in range(8):
        b, p = c // 2, c % 2
        out[b, p * TOK:(p + 1) * TOK, :] = res.results[c]["out_own"]
    if _BUILD_DEBUG:
        kernel._last_results = res.results
    return out


# version 11
# speedup vs baseline: 1.0317x; 1.0317x over previous
import numpy as np
from contextlib import ExitStack

import concourse.bass as bass
import concourse.mybir as mybir
from concourse.bass_utils import run_bass_kernel_spmd

F32 = mybir.dt.float32
BF16 = mybir.dt.bfloat16
I32 = mybir.dt.int32
ALU = mybir.AluOpType
ACTF = mybir.ActivationFunctionType

B, L, D = 4, 4096, 2048
H, DK, DV = 4, 512, 256
KD, VD, LOW = 2048, 1024, 16
CH = 64
SEG, NSEG = 512, 8
PAIRS = SEG // 128
QF = 2 * DK
VF = 2 * DV
TOK = 2048
INTER, NLUT = 5504, 16
NIC = INTER // 128
SH_G = 256
SH_D = 256
OFF_U = SH_G * INTER
OFF_D = 2 * SH_G * INTER
SSZ = 2 * SH_G * INTER + INTER * SH_D
SCALE = 1.0 / float(np.sqrt(DK))
EPS = 1e-5

_BUILD_DEBUG = False
_DBG_SEG = 6


class _Cnt(object):
    __slots__ = ("sem", "n")

    def __init__(self, sem):
        self.sem = sem
        self.n = 0

    def tick(self, instr, k=1):
        instr.then_inc(self.sem, k)
        self.n += k
        return self.n


class _Waiter(object):
    def __init__(self):
        self.seen = {}

    def wait(self, eng, cnt, val=None):
        v = cnt.n if val is None else val
        if v <= 0:
            return
        k = (id(eng), id(cnt))
        if self.seen.get(k, -1) >= v:
            return
        eng.wait_ge(cnt.sem, v)
        self.seen[k] = v


def build_nc():
    nc = bass.Bass("TRN2", target_bir_lowering=False)

    d_x = nc.dram_tensor("x_all", [L, D], F32, kind="ExternalInput").ap()
    d_xo = nc.dram_tensor("x_own", [TOK, D], F32, kind="ExternalInput").ap()
    d_wq = nc.dram_tensor("wq", [D, QF], F32, kind="ExternalInput").ap()
    d_wk = nc.dram_tensor("wk", [D, QF], F32, kind="ExternalInput").ap()
    d_wv = nc.dram_tensor("wv", [D, VF], F32, kind="ExternalInput").ap()
    d_wg = nc.dram_tensor("wg", [D, VF], F32, kind="ExternalInput").ap()
    d_wgk1 = nc.dram_tensor("wgk1", [D, LOW], F32, kind="ExternalInput").ap()
    d_wgk2 = nc.dram_tensor("wgk2", [LOW, QF], F32, kind="ExternalInput").ap()
    d_bgk2 = nc.dram_tensor("bgk2v", [1, QF], F32, kind="ExternalInput").ap()
    d_wn2 = nc.dram_tensor("wnorm2", [1, VF], F32, kind="ExternalInput").ap()
    d_wo = nc.dram_tensor("wo", [VD, D], F32, kind="ExternalInput").ap()
    d_lut = nc.dram_tensor("lutv", [1, NLUT], F32, kind="ExternalInput").ap()
    d_msk = nc.dram_tensor("msk", [128, 4], F32, kind="ExternalInput").ap()
    d_u128 = nc.dram_tensor("u128", [128, 128], F32, kind="ExternalInput").ap()
    d_tril = nc.dram_tensor("trilm", [128, 128], F32, kind="ExternalInput").ap()
    d_gidx = nc.dram_tensor("gidxT", [SH_G, INTER], I32, kind="ExternalInput").ap()
    d_uidx = nc.dram_tensor("uidxT", [SH_G, INTER], I32, kind="ExternalInput").ap()
    d_didx = nc.dram_tensor("didxT", [INTER, SH_D], I32, kind="ExternalInput").ap()
    d_out = nc.dram_tensor("out_own", [TOK, D], F32, kind="ExternalOutput").ap()

    dbg = {}
    if _BUILD_DEBUG:
        dbg["q_fm"] = nc.dram_tensor("dbg_q_fm", [128, 8, SEG], BF16, kind="ExternalOutput").ap()
        dbg["k_fm"] = nc.dram_tensor("dbg_k_fm", [128, 8, SEG], BF16, kind="ExternalOutput").ap()
        dbg["v64"] = nc.dram_tensor("dbg_v64", [128, 4, VF], BF16, kind="ExternalOutput").ap()
        dbg["sp"] = nc.dram_tensor("dbg_sp", [128, 4, QF], F32, kind="ExternalOutput").ap()
        dbg["sgw"] = nc.dram_tensor("dbg_sgw", [128, 4, VF], BF16, kind="ExternalOutput").ap()
        dbg["og"] = nc.dram_tensor("dbg_og", [128, 32, VF], BF16, kind="ExternalOutput").ap()
        dbg["rs"] = nc.dram_tensor("dbg_rs", [2 * TOK, VF], BF16, kind="ExternalOutput").ap()
        dbg["x1"] = nc.dram_tensor("dbg_x1", [TOK, D], F32, kind="ExternalOutput").ap()
        dbg["wexp"] = nc.dram_tensor("dbg_wexp", [8, SSZ], BF16, kind="ExternalOutput").ap()
        dbg["ogfm"] = nc.dram_tensor("dbg_ogfm", [128, 8, TOK], BF16, kind="ExternalOutput").ap()
        dbg["r1"] = nc.dram_tensor("dbg_r1", [TOK, D], F32, kind="ExternalOutput").ap()
        dbg["xo01"] = nc.dram_tensor("dbg_xo01", [128, 2, D], F32, kind="ExternalOutput").ap()
        dbg["st1"] = nc.dram_tensor("dbg_st1", [128, 16, 12], F32, kind="ExternalOutput").ap()
        dbg["eb"] = nc.dram_tensor("dbg_eb", [128, 4, 8, 128], F32, kind="ExternalOutput").ap()
        dbg["ms"] = nc.dram_tensor("dbg_ms", [128, 4, 8], F32, kind="ExternalOutput").ap()

    xbd = nc.dram_tensor("xbi", [L, D], BF16)
    wqb = nc.dram_tensor("wqb", [D, QF], BF16)
    wkb = nc.dram_tensor("wkb", [D, QF], BF16)
    wvb = nc.dram_tensor("wvb", [D, VF], BF16)
    wgb = nc.dram_tensor("wgb", [D, VF], BF16)
    wlb = nc.dram_tensor("wlb", [D, LOW], BF16)
    w2b = nc.dram_tensor("w2b", [LOW, QF], BF16)
    wob = nc.dram_tensor("wob", [VD, D], BF16)
    ogb = nc.dram_tensor("ogb", [4 * TOK, VF], BF16)
    rsout = nc.dram_tensor("rsout", [2 * TOK, VF], BF16)
    wsh = nc.dram_tensor("wsh", [SSZ], BF16)
    wfull = nc.dram_tensor("wfull", [8, SSZ], BF16)
    x1f = nc.dram_tensor("x1f", [TOK, D], F32)
    x1b = nc.dram_tensor("x1b", [TOK, D], BF16)

    st = ExitStack()
    with st:
        c_sy = _Cnt(st.enter_context(nc.semaphore("c_sy")))
        c_gp = _Cnt(st.enter_context(nc.semaphore("c_gp")))
        c_pe = _Cnt(st.enter_context(nc.semaphore("c_pe")))
        c_dv = _Cnt(st.enter_context(nc.semaphore("c_dv")))
        c_ac = _Cnt(st.enter_context(nc.semaphore("c_ac")))
        c_cc = _Cnt(st.enter_context(nc.semaphore("c_cc")))
        c_sq = _Cnt(st.enter_context(nc.semaphore("c_sq")))
        W = _Waiter()
        free = {}

        def fwait(eng, name):
            e = free.get(name)
            if e is not None:
                W.wait(eng, e[0], e[1])

        def fset(name, cnt, val=None):
            free[name] = (cnt, cnt.n if val is None else val)

        sb_wn2 = st.enter_context(nc.sbuf_tensor("sb_wn2", [128, VF], F32))
        sb_bgk = st.enter_context(nc.sbuf_tensor("sb_bgk", [128, QF], F32))
        sb_lut = st.enter_context(nc.sbuf_tensor("sb_lut", [128, NLUT], F32))
        sb_msk = st.enter_context(nc.sbuf_tensor("sb_msk", [128, 4], F32))
        sb_u = st.enter_context(nc.sbuf_tensor("sb_u", [128, 128], F32))
        sb_tril = st.enter_context(nc.sbuf_tensor("sb_tril", [128, 128], F32))
        sb_eps = st.enter_context(nc.sbuf_tensor("sb_eps", [128, 1], F32))
        sb_scr = st.enter_context(nc.sbuf_tensor("sb_scr", [128, 8], F32))
        EW = 2816
        e_i32 = st.enter_context(nc.sbuf_tensor("e_i32", [128, EW], I32))
        e_bf = st.enter_context(nc.sbuf_tensor("e_bf", [128, EW], BF16))
        e_acc = st.enter_context(nc.sbuf_tensor("e_acc", [128, EW], BF16))
        e_tmp = st.enter_context(nc.sbuf_tensor("e_tmp", [128, EW], BF16))

        blk = st.enter_context(nc.Block())

        def p0_sync(sy):
            c_sy.tick(sy.dma_start(out=sb_wn2[:], in_=d_wn2.partition_broadcast(128)), 16)
            c_sy.tick(sy.dma_start(out=sb_bgk[:], in_=d_bgk2.partition_broadcast(128)), 16)
            c_sy.tick(sy.dma_start(out=sb_lut[:], in_=d_lut.partition_broadcast(128)), 16)
            c_sy.tick(sy.dma_start(out=sb_msk[:], in_=d_msk), 16)
            c_sy.tick(sy.dma_start(out=sb_u[:], in_=d_u128), 16)
            c_sy.tick(sy.dma_start(out=sb_tril[:], in_=d_tril), 16)
        blk.sync(p0_sync)

        tok_cast_m = {}

        def p0_gp(gp):
            c_gp.tick(gp.dma_start(out=xbd[0:SEG, :], in_=d_x[0:SEG, :]), 16)
            tok_cast_m["x0"] = c_gp.n
            for dst, src in ((wqb, d_wq), (wkb, d_wk)):
                c_gp.tick(gp.dma_start(out=dst[:, :], in_=src), 16)
            tok_cast_m["qk"] = c_gp.n
            for dst, src in ((wvb, d_wv), (wgb, d_wg), (wlb, d_wgk1),
                             (w2b, d_wgk2)):
                c_gp.tick(gp.dma_start(out=dst[:, :], in_=src), 16)
            tok_cast_m["vg"] = c_gp.n
            for s in range(1, NSEG):
                c_gp.tick(gp.dma_start(out=xbd[s * SEG:(s + 1) * SEG, :],
                                       in_=d_x[s * SEG:(s + 1) * SEG, :]), 16)
                tok_cast_m["x%d" % s] = c_gp.n
            c_gp.tick(gp.dma_start(out=wob[:, :], in_=d_wo), 16)
        blk.gpsimd(p0_gp)
        tok_casts = c_gp.n

        def p0_dve(v):
            v.memset(sb_eps[:], EPS)
            c_dv.tick(v.engine_nop())
        blk.vector(p0_dve)
        tok_eps = c_dv.n

        strips = []
        for dsrc, off0 in ((d_gidx, 0), (d_uidx, OFF_U)):
            for pt in range(2):
                for hf in range(2):
                    base = off0 + pt * 128 * INTER
                    coff = hf * 2752
                    strips.append((
                        dsrc[pt * 128:(pt + 1) * 128, coff:coff + 2752],
                        wsh.ap()[base:base + 128 * INTER].rearrange(
                            "(p n) -> p n", p=128)[:, coff:coff + 2752],
                        2752, None))
        ib0 = 0
        for kblk in (11, 11, 11, 10):
            base = OFF_D + ib0 * 128 * SH_D
            strips.append((
                d_didx[ib0 * 128:(ib0 + kblk) * 128, :].rearrange(
                    "(a p) n -> p a n", p=128),
                wsh.ap()[base:base + kblk * 128 * SH_D].rearrange(
                    "(a p n) -> p a n", p=128, n=SH_D),
                kblk * SH_D, kblk))
            ib0 += kblk
        n_strips = len(strips)

        def emit_strip(si):
            src, dst, width, kblk = strips[si]

            def s_sync(sy):
                fwait(sy, "e_i32")
                io = e_i32[:, 0:width]
                if kblk is not None:
                    io = io.rearrange("p (a n) -> p a n", n=SH_D)
                c_sy.tick(sy.dma_start(out=io, in_=src), 16)
            blk.sync(s_sync)
            tok_ld = c_sy.n

            def s_dve(v):
                W.wait(v, c_sy, tok_ld)
                W.wait(v, c_dv, tok_eps)
                fwait(v, "e_acc")
                ins = v.tensor_copy(e_bf[:, 0:width], e_i32[:, 0:width])
                c_dv.tick(ins)
                fset("e_i32", c_dv)
                v.tensor_copy(sb_scr[:, 0:1], sb_lut[:, 0:1])
                v.tensor_scalar(e_acc[:, 0:width], e_bf[:, 0:width], 0.0,
                                sb_lut[:, 0:1], ALU.is_equal, ALU.mult)
                for j in range(1, NLUT):
                    v.tensor_scalar(e_tmp[:, 0:width], e_bf[:, 0:width], float(j),
                                    sb_lut[:, j:j + 1], ALU.is_equal, ALU.mult)
                    v.tensor_copy(sb_scr[:, 0:1], sb_lut[:, 0:1])
                    ins = v.tensor_add(e_acc[:, 0:width], e_acc[:, 0:width],
                                       e_tmp[:, 0:width])
                c_dv.tick(ins)
            blk.vector(s_dve)
            tok_exp_dve = c_dv.n

            def s_gp(gp):
                W.wait(gp, c_dv, tok_exp_dve)
                io = e_acc[:, 0:width]
                if kblk is not None:
                    io = io.rearrange("p (a n) -> p a n", n=SH_D)
                c_gp.tick(gp.dma_start(out=dst, in_=io), 16)
                fset("e_acc", c_gp)
            blk.gpsimd(s_gp)

        def emit_ag():
            wsh_done = c_gp.n

            def g(gp):
                W.wait(gp, c_gp, wsh_done)
                c_cc.tick(gp.collective_compute(
                    "AllGather", ALU.bypass,
                    replica_groups=[list(range(8))],
                    ins=[wsh.ap().opt()],
                    outs=[wfull.ap().opt()]))
            blk.gpsimd(g)

            if _BUILD_DEBUG:
                ag_done = c_cc.n

                def gdump(gp, ag_done=ag_done):
                    W.wait(gp, c_cc, ag_done)
                    c_gp.tick(gp.dma_start(out=dbg["wexp"], in_=wfull[:, :]), 16)
                blk.gpsimd(gdump)

        emit_strip(0)
        emit_strip(1)
        strip_next = 2

        att = ExitStack()
        x_fm = att.enter_context(nc.sbuf_tensor("x_fm", [128, 16, SEG], BF16))
        x_st = att.enter_context(nc.sbuf_tensor("x_st", [128, 4, D], BF16))
        wqt = att.enter_context(nc.sbuf_tensor("wqt", [128, 4, 16, 128], BF16))
        wv_sb = att.enter_context(nc.sbuf_tensor("wv_sb", [128, 16, VF], BF16))
        wg_sb = att.enter_context(nc.sbuf_tensor("wg_sb", [128, 16, VF], BF16))
        wl_sb = att.enter_context(nc.sbuf_tensor("wl_sb", [128, 16, LOW], BF16))
        w2_sb = att.enter_context(nc.sbuf_tensor("w2_sb", [LOW, QF], BF16))
        q_fm = att.enter_context(nc.sbuf_tensor("q_fm", [128, 8, SEG], BF16))
        k_fm = att.enter_context(nc.sbuf_tensor("k_fm", [128, 8, SEG], BF16))
        v128 = att.enter_context(nc.sbuf_tensor("v128", [128, 4, VF], BF16))
        sp_tok = att.enter_context(nc.sbuf_tensor("sp_tok", [128, 4, QF], F32))
        gk_t = att.enter_context(nc.sbuf_tensor("gk_t", [128, QF], F32))
        gk_s = att.enter_context(nc.sbuf_tensor("gk_s", [128, QF], F32))
        sgw = att.enter_context(nc.sbuf_tensor("sgw", [128, 4, VF], BF16))
        low_fm = att.enter_context(nc.sbuf_tensor("low_fm", [16, SEG], BF16))
        eb = att.enter_context(nc.sbuf_tensor("eb", [128, 8, 128], F32))
        emb = att.enter_context(nc.sbuf_tensor("emb", [128, 8, 128], F32))
        qe = att.enter_context(nc.sbuf_tensor("qe", [128, 2, 4, 128], BF16))
        kdt = att.enter_context(nc.sbuf_tensor("kdt", [128, 2, 4, 128], BF16))
        kdT = att.enter_context(nc.sbuf_tensor("kdT", [128, 2, 4, 128], BF16))
        abf = att.enter_context(nc.sbuf_tensor("abf", [128, 128], BF16))
        s32 = att.enter_context(nc.sbuf_tensor("s32", [128, 2, 4, DV], F32))
        sbf = att.enter_context(nc.sbuf_tensor("sbf", [128, 2, 4, DV], BF16))
        ebl = att.enter_context(nc.sbuf_tensor("ebl", [128, 2, 4], F32))
        ms_t = att.enter_context(nc.sbuf_tensor("ms_t", [128, 8], F32))
        sqs = att.enter_context(nc.sbuf_tensor("sqs", [128, DV], BF16))
        og_sg = att.enter_context(nc.sbuf_tensor("og_sg", [128, 4, VF], BF16))
        og_ms = att.enter_context(nc.sbuf_tensor("og_ms", [128, 2, 2, VF], BF16))

        ps_a = att.enter_context(nc.psum_tensor("ps_a", [128, 2, 512], F32))
        ps_bg = att.enter_context(nc.psum_tensor("ps_bg", [128, 8, 128], F32))
        ps_A = att.enter_context(nc.psum_tensor("ps_A", [128, 128], F32))
        ps_o = att.enter_context(nc.psum_tensor("ps_o", [128, DV], F32))
        ps_p2 = att.enter_context(nc.psum_tensor("ps_p2", [128, 4, DV], F32))

        def a_init(v):
            v.memset(s32[:], 0.0)
            v.memset(ebl[:], 1.0)
            c_dv.tick(v.engine_nop())
        blk.vector(a_init)

        for seg in range(NSEG):
            t0 = seg * SEG
            half = seg // 4

            def l_sync(sy):
                W.wait(sy, c_gp, tok_cast_m["x%d" % seg])
                c_sy.tick(sy.dma_start(
                    out=x_st[:, :, :],
                    in_=xbd[t0:t0 + SEG, :].rearrange("(a p) n -> p a n",
                                                      p=128)), 16)
                W.wait(sy, c_sy, c_sy.n)
                fwait(sy, "x_fm")
                for t4 in range(4):
                    for dc in range(16):
                        c_sy.tick(sy.dma_start(
                            out=x_fm[:, dc, t4 * 128:(t4 + 1) * 128],
                            in_=x_st[:, t4, dc * 128:(dc + 1) * 128],
                            transpose=True), 16)
            blk.sync(l_sync)
            tok_xfm = c_sy.n

            def vg_sc(sc):
                W.wait(sc, c_gp, tok_cast_m["vg"])
                fwait(sc, "wslab")
                for dst, srcb in ((wv_sb, wvb), (wg_sb, wgb), (wl_sb, wlb)):
                    c_sq.tick(sc.dma_start(
                        out=dst[:],
                        in_=srcb[:, :].rearrange("(c p) n -> p c n", p=128)), 16)
                c_sq.tick(sc.dma_start(out=w2_sb[:], in_=w2b[:, :]), 16)
            blk.scalar(vg_sc)
            tok_loads_sq = c_sq.n

            pe_toks = {}
            rot = [0]

            qk_list = []
            for qc_ in range(8):
                qk_list.append(("q", wqb, qc_))
                qk_list.append(("k", wkb, qc_))
            tok_ws = []

            def emit_qk_load(gi):
                kind, srcb, qc = qk_list[gi]
                wb = gi % 4

                def qk_load(sy, srcb=srcb, qc=qc, wb=wb):
                    W.wait(sy, c_gp, tok_cast_m["qk"])
                    fwait(sy, "wqt%d" % wb)
                    c_sy.tick(sy.dma_start(
                        out=wqt[:, wb],
                        in_=srcb[:, :].rearrange("(c p) n -> p c n", p=128)[
                            :, :, qc * 128:(qc + 1) * 128]), 16)
                blk.sync(qk_load)
                tok_ws.append(c_sy.n)

            def emit_qk(gi):
                kind, srcb, qc = qk_list[gi]
                wb = gi % 4
                b_ = rot[0] % 2
                rot[0] += 1

                def qk_pe(pe, gi=gi, wb=wb, b_=b_):
                    W.wait(pe, c_sy, tok_ws[min(gi + 1, 15)])
                    W.wait(pe, c_sy, tok_xfm)
                    fwait(pe, "ps_a%d" % b_)
                    for dc in range(16):
                        ins = pe.matmul(ps_a[:, b_, :],
                                        wqt[:, wb, dc, :],
                                        x_fm[:, dc, :],
                                        start=(dc == 0), stop=(dc == 15))
                    c_pe.tick(ins)
                    fset("wqt%d" % wb, c_pe)
                blk.tensor(qk_pe)
                tok_mm = c_pe.n

                if kind == "q":
                    def ev_q(v, qc=qc, b_=b_, tok_mm=tok_mm):
                        W.wait(v, c_pe, tok_mm)
                        if _BUILD_DEBUG and qc == 0:
                            fwait(v, "pj_dump")
                        ins = v.tensor_copy(q_fm[:, qc, :], ps_a[:, b_, :])
                        c_dv.tick(ins)
                        fset("ps_a%d" % b_, c_dv)
                    blk.vector(ev_q)
                else:
                    def ev_k(sc, qc=qc, b_=b_, tok_mm=tok_mm):
                        W.wait(sc, c_pe, tok_mm)
                        if qc == 0:
                            fwait(sc, "k_fm")
                            if _BUILD_DEBUG:
                                fwait(sc, "pj_dump")
                        ins = sc.copy(k_fm[:, qc, :], ps_a[:, b_, :])
                        c_ac.tick(ins)
                        fset("ps_a%d" % b_, c_ac)
                    blk.scalar(ev_k)

            def emit_vg(kind, wsl, tc):
                b_ = rot[0] % 2
                rot[0] += 1

                def vg_pe(pe, wsl=wsl, tc=tc, b_=b_):
                    W.wait(pe, c_sy, tok_xfm)
                    W.wait(pe, c_sq, tok_loads_sq)
                    fwait(pe, "ps_a%d" % b_)
                    for dc in range(16):
                        ins = pe.matmul(ps_a[:, b_, :],
                                        x_fm[:, dc, tc * 128:(tc + 1) * 128],
                                        wsl[:, dc, :],
                                        start=(dc == 0), stop=(dc == 15))
                    c_pe.tick(ins)
                blk.tensor(vg_pe)
                tok_mm = c_pe.n

                if kind == "v":
                    def ev_v(v, tc=tc, b_=b_, tok_mm=tok_mm):
                        W.wait(v, c_pe, tok_mm)
                        if tc == 0:
                            fwait(v, "v128pe")
                            if _BUILD_DEBUG:
                                fwait(v, "pj_dump")
                        ins = v.tensor_copy(v128[:, tc, :], ps_a[:, b_, :])
                        c_dv.tick(ins)
                        fset("ps_a%d" % b_, c_dv)
                    blk.vector(ev_v)
                else:
                    def ev_g(sc, tc=tc, b_=b_, tok_mm=tok_mm):
                        W.wait(sc, c_pe, tok_mm)
                        fwait(sc, "sgwf%d" % tc)
                        if _BUILD_DEBUG and tc == 0:
                            fwait(sc, "pj_dump")
                        ins = sc.activation(sgw[:, tc, :], ps_a[:, b_, :],
                                            ACTF.Silu)
                        c_ac.tick(ins)
                        fset("ps_a%d" % b_, c_ac)
                    blk.scalar(ev_g)
                    sg_tok = c_ac.n

                    def sgw_mul(v, tc=tc, sg_tok=sg_tok):
                        W.wait(v, c_ac, sg_tok)
                        ins = v.tensor_mul(sgw[:, tc, :], sgw[:, tc, :],
                                           sb_wn2[:])
                        c_dv.tick(ins)
                    blk.vector(sgw_mul)

            emit_qk_load(0)
            emit_qk_load(1)
            for gi in range(16):
                if gi + 2 < 16:
                    emit_qk_load(gi + 2)
                emit_qk(gi)
            for tc in range(4):
                emit_vg("v", wv_sb, tc)
                emit_vg("g", wg_sb, tc)

            b_low = rot[0] % 2
            rot[0] += 1

            def low_pe(pe, b_=b_low):
                W.wait(pe, c_sy, tok_xfm)
                W.wait(pe, c_sq, tok_loads_sq)
                fwait(pe, "ps_a%d" % b_)
                for dc in range(16):
                    ins = pe.matmul(ps_a[0:16, b_, :],
                                    wl_sb[:, dc, :],
                                    x_fm[:, dc, :],
                                    start=(dc == 0), stop=(dc == 15))
                c_pe.tick(ins)
            blk.tensor(low_pe)
            tok_lowmm = c_pe.n
            fset("wslab", c_pe)
            fset("x_fm", c_pe)

            def low_ev(v, b_=b_low, tok_lowmm=tok_lowmm):
                W.wait(v, c_pe, tok_lowmm)
                fwait(v, "low_fm")
                ins = v.tensor_copy(low_fm[:], ps_a[0:16, b_, :])
                c_dv.tick(ins)
                fset("ps_a%d" % b_, c_dv)
                pe_toks["lowev"] = c_dv.n
            blk.vector(low_ev)

            for tc in range(4):
                def gk_pe(pe, tc=tc):
                    W.wait(pe, c_dv, pe_toks["lowev"])
                    fwait(pe, "ps_bg")
                    ps_gk = ps_bg[:].rearrange("p a n -> p (a n)")
                    pe.matmul(ps_gk[:, 0:512],
                              low_fm[:, tc * 128:(tc + 1) * 128],
                              w2_sb[:, 0:512], start=True, stop=True)
                    ins = pe.matmul(ps_gk[:, 512:1024],
                                    low_fm[:, tc * 128:(tc + 1) * 128],
                                    w2_sb[:, 512:1024], start=True, stop=True)
                    c_pe.tick(ins)
                blk.tensor(gk_pe)
                tok_gkmm = c_pe.n

                def gk_dve(v, tok_gkmm=tok_gkmm):
                    W.wait(v, c_pe, tok_gkmm)
                    fwait(v, "gk_t")
                    ins = v.tensor_add(gk_t[:],
                                       ps_bg[:].rearrange("p a n -> p (a n)"),
                                       sb_bgk[:])
                    c_dv.tick(ins)
                    fset("ps_bg", c_dv)
                blk.vector(gk_dve)
                tok_gka = c_dv.n

                def gk_act(sc, tc=tc, tok_gka=tok_gka):
                    W.wait(sc, c_dv, tok_gka)
                    fwait(sc, "sp_tok")
                    if _BUILD_DEBUG and tc == 0:
                        fwait(sc, "pj_dump")
                    sc.activation(gk_s[:], gk_t[:], ACTF.Sigmoid)
                    sc.copy(sb_scr[:, 1:2], sb_eps[:])
                    ins = sc.activation(sp_tok[:, tc, :], gk_s[:], ACTF.Ln)
                    tk = c_ac.tick(ins)
                    fset("gk_t", c_ac, tk)
                blk.scalar(gk_act)
            tok_sp = c_ac.n
            free.pop("low_fm", None)
            fset("low_fm", c_pe)

            if _BUILD_DEBUG and seg == _DBG_SEG:
                def dbg_gp(gp):
                    gp.wait_ge(c_ac.sem, c_ac.n)
                    gp.wait_ge(c_dv.sem, c_dv.n)
                    c_gp.tick(gp.dma_start(out=dbg["q_fm"], in_=q_fm[:]), 16)
                    c_gp.tick(gp.dma_start(out=dbg["k_fm"], in_=k_fm[:]), 16)
                    c_gp.tick(gp.dma_start(out=dbg["v64"], in_=v128[:]), 16)
                    c_gp.tick(gp.dma_start(out=dbg["sp"], in_=sp_tok[:]), 16)
                    c_gp.tick(gp.dma_start(out=dbg["sgw"], in_=sgw[:]), 16)
                    fset("pj_dump", c_gp)
                blk.gpsimd(dbg_gp)

            for j in range(PAIRS):
                def b_pe(pe, j=j):
                    W.wait(pe, c_ac, tok_sp)
                    fwait(pe, "ps_bg")
                    for fc in range(8):
                        ins = pe.matmul(ps_bg[:, fc, :],
                                        sp_tok[:, j, fc * 128:(fc + 1) * 128],
                                        sb_u[:], start=True, stop=True)
                    c_pe.tick(ins)
                blk.tensor(b_pe)
                tok_bmm = c_pe.n

                def b_act(sc, tok_bmm=tok_bmm):
                    W.wait(sc, c_pe, tok_bmm)
                    fwait(sc, "eb")
                    if _BUILD_DEBUG:
                        fwait(sc, "ebd")
                    sc.activation(eb[:].rearrange("p a n -> p (a n)"),
                                  ps_bg[:].rearrange("p a n -> p (a n)"), ACTF.Exp)
                    ins = sc.activation(emb[:].rearrange("p a n -> p (a n)"),
                                        ps_bg[:].rearrange("p a n -> p (a n)"),
                                        ACTF.Exp, scale=-1.0)
                    c_ac.tick(ins)
                    fset("ps_bg", c_ac)
                blk.scalar(b_act)
                tok_expo = c_ac.n

                def qk_dve(v, j=j, tok_expo=tok_expo):
                    W.wait(v, c_ac, tok_expo)
                    fwait(v, "qekd")
                    for h_ in range(2):
                        v.scalar_tensor_tensor(
                            qe[:, h_, :, :],
                            q_fm[:, 4 * h_:4 * h_ + 4, j * 128:(j + 1) * 128],
                            SCALE, eb[:, 4 * h_:4 * h_ + 4, :], ALU.mult, ALU.mult)
                        v.tensor_mul(
                            kdt[:, h_, :, :],
                            k_fm[:, 4 * h_:4 * h_ + 4, j * 128:(j + 1) * 128],
                            emb[:, 4 * h_:4 * h_ + 4, :])
                    c_dv.tick(v.engine_nop())
                blk.vector(qk_dve)
                tok_qekd = c_dv.n

                if _BUILD_DEBUG and seg == _DBG_SEG:
                    def dbg_eb_gp(gp, j=j, tok_qekd=tok_qekd):
                        W.wait(gp, c_dv, tok_qekd)
                        c_gp.tick(gp.dma_start(out=dbg["eb"][:, j, :, :],
                                               in_=eb[:]), 16)
                        fset("ebd", c_gp)
                    blk.gpsimd(dbg_eb_gp)

                def kdT_sync(sy, tok_qekd=tok_qekd):
                    W.wait(sy, c_dv, tok_qekd)
                    fwait(sy, "kdT")
                    fwait(sy, "kdT_dv")
                    for h_ in range(2):
                        for dc in range(4):
                            c_sy.tick(sy.dma_start(
                                out=kdT[:, h_, dc, :],
                                in_=kdt[:, h_, dc, :], transpose=True), 16)
                blk.sync(kdT_sync)
                tok_kdT = c_sy.n

                fset("kdT_dv", c_sy, tok_kdT)

                for h_ in range(2):
                    def sbf_dve(v, h_=h_):
                        fwait(v, "sbf%d" % h_)
                        for dc in range(4):
                            ins = v.tensor_scalar(
                                sbf[:, h_, dc, :], s32[:, h_, dc, :],
                                ebl[:, h_, dc:dc + 1], None, ALU.mult)
                        c_dv.tick(ins)
                    blk.vector(sbf_dve)
                    tok_sbf = c_dv.n

                    def A_pe(pe, h_=h_, tok_qekd=tok_qekd):
                        W.wait(pe, c_dv, tok_qekd)
                        fwait(pe, "ps_A")
                        for dc in range(4):
                            ins = pe.matmul(ps_A[:, :],
                                            kdt[:, h_, dc, :],
                                            qe[:, h_, dc, :],
                                            start=(dc == 0), stop=(dc == 3))
                        c_pe.tick(ins)
                    blk.tensor(A_pe)
                    tok_A = c_pe.n

                    def Am_dve(v, tok_A=tok_A):
                        W.wait(v, c_pe, tok_A)
                        fwait(v, "abf")
                        ins = v.tensor_mul(abf[:, :], ps_A[:, :], sb_tril[:])
                        c_dv.tick(ins)
                        fset("ps_A", c_dv)
                    blk.vector(Am_dve)
                    tok_Am = c_dv.n

                    def o_pe(pe, h_=h_, j=j, tok_sbf=tok_sbf, tok_Am=tok_Am):
                        W.wait(pe, c_dv, tok_sbf)
                        fwait(pe, "ps_o")
                        for dc in range(4):
                            pe.matmul(ps_o[:, :],
                                      qe[:, h_, dc, :],
                                      sbf[:, h_, dc, :],
                                      start=(dc == 0), stop=False)
                        W.wait(pe, c_dv, tok_Am)
                        ins = pe.matmul(ps_o[:, :], abf[:, :],
                                        v128[:, j, h_ * DV:(h_ + 1) * DV],
                                        start=False, stop=True)
                        c_pe.tick(ins)
                    blk.tensor(o_pe)
                    tok_o = c_pe.n
                    fset("abf", c_pe, tok_o)
                    fset("sbf%d" % h_, c_pe, tok_o)

                    mcol = 4 * h_

                    def rms_act(sc, mcol=mcol, tok_o=tok_o):
                        W.wait(sc, c_pe, tok_o)
                        fwait(sc, "ms%d" % mcol)
                        if _BUILD_DEBUG:
                            fwait(sc, "msd")
                        sc.activation(sqs[:], ps_o[:, :], ACTF.Square,
                                      accum_out=ms_t[:, mcol:mcol + 1])
                        sc.copy(ms_t[:, 3:4], sb_eps[:])
                        sc.activation(ms_t[:, mcol + 1:mcol + 2],
                                      ms_t[:, mcol:mcol + 1], ACTF.Ln,
                                      scale=1.0 / DV, bias=sb_eps[:])
                        sc.copy(ms_t[:, 3:4], sb_eps[:])
                        ins = sc.activation(ms_t[:, mcol + 2:mcol + 3],
                                            ms_t[:, mcol + 1:mcol + 2],
                                            ACTF.Exp, scale=-0.5)
                        c_ac.tick(ins)
                    blk.scalar(rms_act)
                    tok_rstd = c_ac.n

                    def og_dve(v, h_=h_, j=j, mcol=mcol, tok_rstd=tok_rstd):
                        W.wait(v, c_ac, tok_rstd)
                        if _BUILD_DEBUG:
                            fwait(v, "ogsg%d" % j)
                        ins = v.scalar_tensor_tensor(
                            og_sg[:, j, h_ * DV:(h_ + 1) * DV],
                            ps_o[:, :], ms_t[:, mcol + 2:mcol + 3],
                            sgw[:, j, h_ * DV:(h_ + 1) * DV],
                            ALU.mult, ALU.mult)
                        c_dv.tick(ins)
                        fset("ps_o", c_dv)
                        fset("ms%d" % mcol, c_dv)
                        if h_ == 1:
                            fset("sgwf%d" % j, c_dv)
                    blk.vector(og_dve)

                    def p2_pe(pe, h_=h_, j=j, tok_kdT=tok_kdT):
                        W.wait(pe, c_sy, tok_kdT)
                        fwait(pe, "ps_p2")
                        for dc in range(4):
                            ins = pe.matmul(
                                ps_p2[:, dc, :], kdT[:, h_, dc, :],
                                v128[:, j, h_ * DV:(h_ + 1) * DV],
                                start=True, stop=True)
                        c_pe.tick(ins)
                    blk.tensor(p2_pe)
                    tok_p2 = c_pe.n

                    def s_dve(v, h_=h_, tok_p2=tok_p2):
                        W.wait(v, c_pe, tok_p2)
                        for dc in range(4):
                            v.scalar_tensor_tensor(
                                s32[:, h_, dc, :], s32[:, h_, dc, :],
                                ebl[:, h_, dc:dc + 1], ps_p2[:, dc, :],
                                ALU.mult, ALU.add)
                        v.tensor_copy(
                            ebl[:, h_, :],
                            eb[:, 4 * h_:4 * h_ + 4, 127:128]
                            .rearrange("p a n -> p (a n)"))
                        c_dv.tick(v.engine_nop())
                        fset("ps_p2", c_dv)
                    blk.vector(s_dve)
                fset("eb", c_dv)
                fset("qekd", c_pe)
                fset("kdT", c_pe)

                if _BUILD_DEBUG and seg == _DBG_SEG:
                    ms_tok = c_dv.n

                    def dbg_ms_gp(gp, j=j, ms_tok=ms_tok):
                        W.wait(gp, c_dv, ms_tok)
                        c_gp.tick(gp.dma_start(out=dbg["ms"][:, j, :],
                                               in_=ms_t[:]), 16)
                        fset("msd", c_gp)
                    blk.gpsimd(dbg_ms_gp)

                ttl = (seg % 4) * 4 + j

                def ogm_dve(v, j=j, half=half):
                    fwait(v, "og_ms%d" % (j % 2))
                    for bk in range(2):
                        ins = v.tensor_scalar(
                            og_ms[:, j % 2, bk, :],
                            og_sg[:, j, :],
                            sb_msk[:, 2 * half + bk:2 * half + bk + 1],
                            None, ALU.mult)
                    c_dv.tick(ins)
                blk.vector(ogm_dve)
                tok_ogm = c_dv.n

                def ogm_sc(sc, j=j, half=half, ttl=ttl, tok_ogm=tok_ogm):
                    W.wait(sc, c_dv, tok_ogm)
                    for bk in range(2):
                        base = (2 * half + bk) * TOK + ttl * 128
                        c_ac.tick(sc.dma_start(
                            out=ogb[base:base + 128, :],
                            in_=og_ms[:, j % 2, bk, :]), 16)
                    fset("og_ms%d" % (j % 2), c_ac)
                blk.scalar(ogm_sc)

                if _BUILD_DEBUG:
                    ttg = seg * 4 + j

                    def dbg_og_gp(gp, j=j, ttg=ttg, tok_ogm=tok_ogm):
                        W.wait(gp, c_dv, tok_ogm)
                        c_gp.tick(gp.dma_start(
                            out=dbg["og"][:, ttg, :],
                            in_=og_sg[:, j, :]), 16)
                        fset("ogsg%d" % j, c_gp)
                    blk.gpsimd(dbg_og_gp)
            fset("k_fm", c_dv)
            fset("sp_tok", c_pe)
            fset("v128pe", c_pe)

            while strip_next < n_strips and strip_next <= 2 + 2 * seg:
                emit_strip(strip_next)
                strip_next += 1
            if strip_next == n_strips:
                emit_ag()
                strip_next += 1
        if strip_next <= n_strips:
            while strip_next < n_strips:
                emit_strip(strip_next)
                strip_next += 1
            emit_ag()
        tok_ag = c_cc.n
        tok_ogb_all = c_ac.n

        def rs_gp(gp):
            W.wait(gp, c_ac, tok_ogb_all)
            c_cc.tick(gp.collective_compute(
                "ReduceScatter", ALU.add,
                replica_groups=[[0, 1], [2, 3], [4, 5], [6, 7]],
                ins=[ogb.ap().opt()],
                outs=[rsout.ap().opt()]))
        blk.gpsimd(rs_gp)
        tok_rs = c_cc.n

        att.close()

        wo_ph = ExitStack()
        rs_sb = wo_ph.enter_context(nc.sbuf_tensor("rs_sb", [128, 32, VF], BF16))
        og_fm = wo_ph.enter_context(nc.sbuf_tensor("og_fm", [128, 8, TOK], BF16))
        wo_sb = wo_ph.enter_context(nc.sbuf_tensor("wo_sb", [128, 8, D], BF16))
        xo_st = wo_ph.enter_context(nc.sbuf_tensor("xo_st", [128, 2, D], F32))
        r1_t = wo_ph.enter_context(nc.sbuf_tensor("r1_t", [128, 2, D], F32))
        dmp1 = wo_ph.enter_context(nc.sbuf_tensor("dmp1", [128, 2, D], BF16))
        st1 = wo_ph.enter_context(nc.sbuf_tensor("st1", [128, 2, 12], F32))
        x1_st = wo_ph.enter_context(nc.sbuf_tensor("x1_st", [128, 2, D], F32))
        x1b_st = wo_ph.enter_context(nc.sbuf_tensor("x1b_st", [128, 2, D], BF16))
        ps_w = wo_ph.enter_context(nc.psum_tensor("ps_w", [128, 2, D], F32))

        def wo_load(gp):
            W.wait(gp, c_cc, tok_rs)
            c_gp.tick(gp.dma_start(
                out=rs_sb[:],
                in_=rsout[:, :].rearrange("(a p) n -> p a n", p=128)), 16)
        blk.gpsimd(wo_load)
        tok_rsld = c_gp.n

        if _BUILD_DEBUG:
            def dbg_rs_gp(gp):
                W.wait(gp, c_cc, tok_rs)
                c_gp.tick(gp.dma_start(out=dbg["rs"], in_=rsout[:, :]), 16)
            blk.gpsimd(dbg_rs_gp)

        def wo_sync(sy):
            W.wait(sy, c_gp, tok_rsld)
            c_sy.tick(sy.dma_start(
                out=wo_sb[:],
                in_=wob[:, :].rearrange("(c p) n -> p c n", p=128)), 16)
            for blk_i in range(2):
                for tt in range(16):
                    for vc in range(4):
                        c_sy.tick(sy.dma_start(
                            out=og_fm[:, blk_i * 4 + vc, tt * 128:(tt + 1) * 128],
                            in_=rs_sb[:, blk_i * 16 + tt, vc * 128:(vc + 1) * 128],
                            transpose=True), 16)
            for buf in range(2):
                c_sy.tick(sy.dma_start(out=xo_st[:, buf, :],
                                       in_=d_xo[buf * 128:(buf + 1) * 128, :]), 16)
        blk.sync(wo_sync)
        tok_ogfm = c_sy.n
        free["xo0"] = (c_sy, tok_ogfm)
        free["xo1"] = (c_sy, tok_ogfm)

        if _BUILD_DEBUG:
            def dbg_ogfm_gp(gp):
                gp.wait_ge(c_sy.sem, tok_ogfm)
                c_gp.tick(gp.dma_start(out=dbg["ogfm"], in_=og_fm[:]), 16)
                c_gp.tick(gp.dma_start(out=dbg["xo01"], in_=xo_st[:]), 16)
                fset("xo_dump", c_gp)
            blk.gpsimd(dbg_ogfm_gp)

        x1_done_toks = []
        for tc in range(16):
            buf = tc % 2

            def wo_pe(pe, tc=tc, buf=buf):
                W.wait(pe, c_sy, tok_ogfm)
                fwait(pe, "ps_w%d" % buf)
                for vdc in range(8):
                    for ns in range(4):
                        ins = pe.matmul(
                            ps_w[:, buf, ns * 512:(ns + 1) * 512],
                            og_fm[:, vdc, tc * 128:(tc + 1) * 128],
                            wo_sb[:, vdc, ns * 512:(ns + 1) * 512],
                            start=(vdc == 0), stop=(vdc == 7))
                c_pe.tick(ins)
            blk.tensor(wo_pe)
            tok_wo = c_pe.n

            def ln1_dve(v, buf=buf, tok_wo=tok_wo):
                W.wait(v, c_pe, tok_wo)
                fwait(v, "xo%d" % buf)
                fwait(v, "r1a%d" % buf)
                if _BUILD_DEBUG:
                    fwait(v, "r1dmp%d" % buf)
                ins = v.tensor_add(r1_t[:, buf, :], ps_w[:, buf, :],
                                   xo_st[:, buf, :])
                c_dv.tick(ins)
                fset("ps_w%d" % buf, c_dv)
            blk.vector(ln1_dve)
            tok_r1 = c_dv.n

            if _BUILD_DEBUG:
                def dbg_r1_gp(gp, tc=tc, buf=buf, tok_r1=tok_r1):
                    W.wait(gp, c_dv, tok_r1)
                    c_gp.tick(gp.dma_start(
                        out=dbg["r1"][tc * 128:(tc + 1) * 128, :],
                        in_=r1_t[:, buf, :]), 16)
                    fset("r1dmp%d" % buf, c_gp)
                blk.gpsimd(dbg_r1_gp)

            def xo_next(sy, tc=tc, buf=buf, tok_r1=tok_r1):
                if tc + 2 < 16:
                    W.wait(sy, c_dv, tok_r1)
                    if _BUILD_DEBUG:
                        fwait(sy, "xo_dump")
                    c_sy.tick(sy.dma_start(
                        out=xo_st[:, buf, :],
                        in_=d_xo[(tc + 2) * 128:(tc + 3) * 128, :]), 16)
                    fset("xo%d" % buf, c_sy)
            blk.sync(xo_next)

            def ln1_act(sc, buf=buf, tok_r1=tok_r1):
                W.wait(sc, c_dv, tok_r1)
                if _BUILD_DEBUG:
                    fwait(sc, "st1dmp%d" % buf)
                sc.activation(dmp1[:, 0, :], r1_t[:, buf, :], ACTF.Identity,
                              accum_out=st1[:, buf, 0:1])
                sc.activation(dmp1[:, 1, :], r1_t[:, buf, :], ACTF.Square,
                              accum_out=st1[:, buf, 1:2])
                sc.copy(st1[:, buf, 2:3], st1[:, buf, 0:1])
                ins = sc.copy(st1[:, buf, 3:4], st1[:, buf, 1:2])
                c_ac.tick(ins)
            blk.scalar(ln1_act)
            tok_st1 = c_ac.n
            fset("r1a%d" % buf, c_ac, tok_st1)

            def ln1b_dve(v, buf=buf, tok_st1=tok_st1):
                W.wait(v, c_ac, tok_st1)
                v.tensor_scalar(st1[:, buf, 5:6], st1[:, buf, 3:4], 1.0 / D,
                                None, ALU.mult)
                v.tensor_mul(st1[:, buf, 6:7], st1[:, buf, 2:3],
                             st1[:, buf, 2:3])
                v.tensor_scalar(st1[:, buf, 4:5], st1[:, buf, 2:3], 1.0 / D,
                                None, ALU.mult)
                ins = v.tensor_scalar(st1[:, buf, 7:8], st1[:, buf, 6:7],
                                      -1.0 / (float(D) * float(D)),
                                      st1[:, buf, 5:6],
                                      ALU.mult, ALU.add)
                c_dv.tick(ins)
            blk.vector(ln1b_dve)
            tok_var = c_dv.n

            def ln1c_act(sc, buf=buf, tok_var=tok_var):
                W.wait(sc, c_dv, tok_var)
                sc.activation(st1[:, buf, 8:9], st1[:, buf, 7:8], ACTF.Ln,
                              bias=sb_eps[:])
                sc.copy(st1[:, buf, 10:11], sb_eps[:])
                ins = sc.activation(st1[:, buf, 9:10], st1[:, buf, 8:9],
                                    ACTF.Exp, scale=-0.5)
                c_ac.tick(ins)
            blk.scalar(ln1c_act)
            tok_rstd1 = c_ac.n

            def ln1d_dve(v, buf=buf, tok_rstd1=tok_rstd1):
                W.wait(v, c_ac, tok_rstd1)
                fwait(v, "x1s%d" % buf)
                v.tensor_scalar(x1_st[:, buf, :], r1_t[:, buf, :],
                                st1[:, buf, 4:5], st1[:, buf, 9:10],
                                ALU.subtract, ALU.mult)
                v.tensor_copy(sb_scr[:, 0:1], sb_eps[:])
                ins = v.tensor_copy(x1b_st[:, buf, :], x1_st[:, buf, :])
                c_dv.tick(ins)
            blk.vector(ln1d_dve)
            tok_x1 = c_dv.n

            def x1_sync(sy, tc=tc, buf=buf, tok_x1=tok_x1):
                W.wait(sy, c_dv, tok_x1)
                c_sy.tick(sy.dma_start(out=x1f[tc * 128:(tc + 1) * 128, :],
                                       in_=x1_st[:, buf, :]), 16)
                c_sy.tick(sy.dma_start(out=x1b[tc * 128:(tc + 1) * 128, :],
                                       in_=x1b_st[:, buf, :]), 16)
                fset("x1s%d" % buf, c_sy)
            blk.sync(x1_sync)
            x1_done_toks.append(c_sy.n)

            if _BUILD_DEBUG:
                def dbg_st1_gp(gp, tc=tc, buf=buf, tok_x1=tok_x1):
                    W.wait(gp, c_dv, tok_x1)
                    c_gp.tick(gp.dma_start(
                        out=dbg["st1"][:, tc, :], in_=st1[:, buf, :]), 16)
                    fset("st1dmp%d" % buf, c_gp)
                blk.gpsimd(dbg_st1_gp)
        tok_x1_all = c_sy.n

        if _BUILD_DEBUG:
            def dbg_x1_gp(gp):
                gp.wait_ge(c_sy.sem, tok_x1_all)
                c_gp.tick(gp.dma_start(out=dbg["x1"], in_=x1f[:, :]), 16)
            blk.gpsimd(dbg_x1_gp)

        wo_ph.close()

        ffn = ExitStack()
        x1fm = ffn.enter_context(nc.sbuf_tensor("x1fm", [128, 16, 512], BF16))
        h_fm = ffn.enter_context(nc.sbuf_tensor("h_fm", [128, NIC, 512], BF16))
        wgu = ffn.enter_context(nc.sbuf_tensor("wgu", [128, 4, 2, 16, 128], BF16))
        wd_sb = ffn.enter_context(nc.sbuf_tensor("wd_sb", [128, 4, 512], BF16))
        x1t = ffn.enter_context(nc.sbuf_tensor("x1t", [128, 2, 4, 512], F32))
        r2 = ffn.enter_context(nc.sbuf_tensor("r2", [128, 4, D], F32))
        sg2 = ffn.enter_context(nc.sbuf_tensor("sg2", [128, 2, 512], BF16))
        dmp2 = ffn.enter_context(nc.sbuf_tensor("dmp2", [128, 2, D], BF16))
        st2 = ffn.enter_context(nc.sbuf_tensor("st2", [128, 2, 12], F32))
        ostg = ffn.enter_context(nc.sbuf_tensor("ostg", [128, 2, D], F32))
        ps_g = ffn.enter_context(nc.psum_tensor("ps_g", [128, 2, 512], F32))
        ps_u = ffn.enter_context(nc.psum_tensor("ps_u", [128, 2, 512], F32))
        ps_d = ffn.enter_context(nc.psum_tensor("ps_d", [128, 4, 512], F32))

        for tt4 in range(4):
            r0 = tt4 * 512

            def f_sync(sy, tt4=tt4, r0=r0):
                W.wait(sy, c_sy, x1_done_toks[4 * tt4 + 3])
                fwait(sy, "x1fm")
                for t4 in range(4):
                    for dc in range(16):
                        c_sy.tick(sy.dma_start(
                            out=x1fm[:, dc, t4 * 128:(t4 + 1) * 128],
                            in_=x1b[r0 + t4 * 128:r0 + (t4 + 1) * 128,
                                    dc * 128:(dc + 1) * 128],
                            transpose=True), 16)
            blk.sync(f_sync)
            tok_x1fm = c_sy.n

            h_toks = {}
            gu_tok_ws = []

            def emit_gu_load(oc):
                buf = oc % 4

                def gu_load(sy, oc=oc, buf=buf):
                    W.wait(sy, c_cc, tok_ag)
                    fwait(sy, "wgu%d" % buf)
                    for gi, off0 in ((0, 0), (1, OFF_U)):
                        for jj in range(2):
                            srcj = wfull[:, off0 + jj * 128 * INTER:
                                         off0 + (jj + 1) * 128 * INTER].rearrange(
                                "c (p n) -> p c n", p=128)[
                                :, :, oc * 128:(oc + 1) * 128]
                            c_sy.tick(sy.dma_start(
                                out=wgu[:, buf, gi, jj::2, :],
                                in_=srcj), 16)
                blk.sync(gu_load)
                gu_tok_ws.append(c_sy.n)

            emit_gu_load(0)
            emit_gu_load(1)
            for oc in range(NIC):
                wb4 = oc % 4
                buf = oc % 2
                if oc + 2 < NIC:
                    emit_gu_load(oc + 2)

                def gu_pe(pe, oc=oc, wb4=wb4, buf=buf):
                    W.wait(pe, c_sy, gu_tok_ws[min(oc + 1, NIC - 1)])
                    W.wait(pe, c_sy, tok_x1fm)
                    fwait(pe, "ps_g%d" % buf)
                    for dc in range(16):
                        ins = pe.matmul(ps_g[:, buf, :],
                                        wgu[:, wb4, 0, dc, :],
                                        x1fm[:, dc, :],
                                        start=(dc == 0), stop=(dc == 15))
                    c_pe.tick(ins)
                    fwait(pe, "ps_u%d" % buf)
                    for dc in range(16):
                        ins = pe.matmul(ps_u[:, buf, :],
                                        wgu[:, wb4, 1, dc, :],
                                        x1fm[:, dc, :],
                                        start=(dc == 0), stop=(dc == 15))
                    c_pe.tick(ins)
                blk.tensor(gu_pe)
                tok_gu = c_pe.n
                fset("wgu%d" % wb4, c_pe, tok_gu)

                def silu_act(sc, buf=buf, tok_gu=tok_gu):
                    W.wait(sc, c_pe, tok_gu - 1)
                    fwait(sc, "sg2%d" % buf)
                    ins = sc.activation(sg2[:, buf, :], ps_g[:, buf, :], ACTF.Silu)
                    c_ac.tick(ins)
                    fset("ps_g%d" % buf, c_ac)
                blk.scalar(silu_act)
                tok_silu = c_ac.n

                def h_dve(v, oc=oc, buf=buf, tok_silu=tok_silu, tok_gu=tok_gu):
                    W.wait(v, c_ac, tok_silu)
                    W.wait(v, c_pe, tok_gu)
                    ins = v.tensor_mul(h_fm[:, oc, :], sg2[:, buf, :],
                                       ps_u[:, buf, :])
                    c_dv.tick(ins)
                    fset("ps_u%d" % buf, c_dv)
                    fset("sg2%d" % buf, c_dv)
                blk.vector(h_dve)
                h_toks[oc] = c_dv.n
            fset("x1fm", c_pe)

            dn_items = [(mq_, ic_) for mq_ in range(4) for ic_ in range(NIC)]
            wd_tok_ws = []

            def emit_wd_load(di):
                mq, ic = dn_items[di]
                wb = di % 4

                def wd_load(sy, mq=mq, ic=ic, wb=wb):
                    fwait(sy, "wd%d" % wb)
                    src = wfull[2 * mq:2 * mq + 2,
                                OFF_D + ic * 128 * SH_D:
                                OFF_D + (ic + 1) * 128 * SH_D].rearrange(
                        "s (p n) -> p s n", p=128)
                    c_sy.tick(sy.dma_start(
                        out=wd_sb[:, wb, :].rearrange(
                            "p (s n) -> p s n", s=2), in_=src), 16)
                blk.sync(wd_load)
                wd_tok_ws.append(c_sy.n)

            x1t_toks = []

            def emit_x1t_load(mq):
                xb = mq % 2

                def x1t_load(sy, mq=mq, xb=xb, r0=r0):
                    fwait(sy, "x1t%d" % xb)
                    c_sy.tick(sy.dma_start(
                        out=x1t[:, xb],
                        in_=x1f[r0:r0 + 512,
                                mq * 512:(mq + 1) * 512].rearrange(
                            "(a p) n -> p a n", p=128)), 16)
                blk.sync(x1t_load)
                x1t_toks.append(c_sy.n)

            emit_wd_load(0)
            emit_wd_load(1)
            emit_x1t_load(0)
            emit_x1t_load(1)
            for mq in range(4):
                for ic in range(NIC):
                    di = mq * NIC + ic
                    wb = di % 4
                    if di + 2 < len(dn_items):
                        emit_wd_load(di + 2)

                    def dn_pe(pe, mq=mq, ic=ic, wb=wb, di=di):
                        W.wait(pe, c_sy,
                               wd_tok_ws[min(di + 1, len(dn_items) - 1)])
                        W.wait(pe, c_dv, h_toks[ic])
                        if ic == 0:
                            for tc in range(4):
                                fwait(pe, "ps_d%d" % tc)
                        for tc in range(4):
                            ins = pe.matmul(ps_d[:, tc, :],
                                            h_fm[:, ic, tc * 128:(tc + 1) * 128],
                                            wd_sb[:, wb, :],
                                            start=(ic == 0), stop=(ic == NIC - 1))
                        c_pe.tick(ins)
                        fset("wd%d" % wb, c_pe)
                    blk.tensor(dn_pe)
                tok_dn = c_pe.n

                def dn_evac(v, mq=mq, tok_dn=tok_dn):
                    W.wait(v, c_pe, tok_dn)
                    W.wait(v, c_sy, x1t_toks[mq])
                    xb = mq % 2
                    for tc in range(4):
                        ins = v.tensor_add(r2[:, tc, mq * 512:(mq + 1) * 512],
                                           ps_d[:, tc, :],
                                           x1t[:, xb, tc, :])
                        c_dv.tick(ins)
                        fset("ps_d%d" % tc, c_dv)
                    fset("x1t%d" % xb, c_dv)
                blk.vector(dn_evac)
                if mq + 2 < 4:
                    emit_x1t_load(mq + 2)
            tok_r2 = c_dv.n

            for tc in range(4):
                buf = tc % 2

                def ln2_act(sc, tc=tc, buf=buf, tok_r2=tok_r2):
                    W.wait(sc, c_dv, tok_r2)
                    fwait(sc, "r2a%d" % buf)
                    sc.activation(dmp2[:, 0, :], r2[:, tc, :], ACTF.Identity,
                                  accum_out=st2[:, buf, 0:1])
                    sc.activation(dmp2[:, 1, :], r2[:, tc, :], ACTF.Square,
                                  accum_out=st2[:, buf, 1:2])
                    sc.copy(st2[:, buf, 2:3], st2[:, buf, 0:1])
                    ins = sc.copy(st2[:, buf, 3:4], st2[:, buf, 1:2])
                    c_ac.tick(ins)
                blk.scalar(ln2_act)
                tok_s2 = c_ac.n

                def ln2_dve(v, buf=buf, tok_s2=tok_s2):
                    W.wait(v, c_ac, tok_s2)
                    v.tensor_scalar(st2[:, buf, 5:6], st2[:, buf, 3:4], 1.0 / D,
                                    None, ALU.mult)
                    v.tensor_mul(st2[:, buf, 6:7], st2[:, buf, 2:3],
                                 st2[:, buf, 2:3])
                    v.tensor_scalar(st2[:, buf, 4:5], st2[:, buf, 2:3], 1.0 / D,
                                    None, ALU.mult)
                    ins = v.tensor_scalar(st2[:, buf, 7:8], st2[:, buf, 6:7],
                                          -1.0 / (float(D) * float(D)),
                                          st2[:, buf, 5:6],
                                          ALU.mult, ALU.add)
                    c_dv.tick(ins)
                blk.vector(ln2_dve)
                tok_v2 = c_dv.n

                def ln2b_act(sc, buf=buf, tok_v2=tok_v2):
                    W.wait(sc, c_dv, tok_v2)
                    sc.activation(st2[:, buf, 8:9], st2[:, buf, 7:8], ACTF.Ln,
                                  bias=sb_eps[:])
                    sc.copy(st2[:, buf, 10:11], sb_eps[:])
                    ins = sc.activation(st2[:, buf, 9:10], st2[:, buf, 8:9],
                                        ACTF.Exp, scale=-0.5)
                    c_ac.tick(ins)
                blk.scalar(ln2b_act)
                tok_rstd2 = c_ac.n

                def out_dve(v, tc=tc, buf=buf, tok_rstd2=tok_rstd2):
                    W.wait(v, c_ac, tok_rstd2)
                    fwait(v, "ostg%d" % buf)
                    ins = v.tensor_scalar(ostg[:, buf, :], r2[:, tc, :],
                                          st2[:, buf, 4:5], st2[:, buf, 9:10],
                                          ALU.subtract, ALU.mult)
                    tk = c_dv.tick(ins)
                    fset("r2a%d" % buf, c_dv, tk)
                blk.vector(out_dve)
                tok_outv = c_dv.n

                def out_sync(sy, tt4=tt4, tc=tc, buf=buf, tok_outv=tok_outv):
                    W.wait(sy, c_dv, tok_outv)
                    rr = tt4 * 512 + tc * 128
                    c_sy.tick(sy.dma_start(out=d_out[rr:rr + 128, :],
                                           in_=ostg[:, buf, :]), 16)
                    fset("ostg%d" % buf, c_sy)
                blk.sync(out_sync)
        ffn.close()
        tok_final = c_sy.n

        def fin_sync(sy):
            sy.wait_ge(c_sy.sem, tok_final)
        blk.sync(fin_sync)

        def fin_gp(gp):
            gp.wait_ge(c_sy.sem, tok_final)
            if _BUILD_DEBUG:
                gp.wait_ge(c_gp.sem, c_gp.n)
        blk.gpsimd(fin_gp)

        build_nc._counters = {"c_sy": c_sy.n, "c_gp": c_gp.n, "c_pe": c_pe.n,
                              "c_dv": c_dv.n, "c_ac": c_ac.n, "c_cc": c_cc.n}

    return nc


def _host_inputs(inputs):
    x = np.ascontiguousarray(np.asarray(inputs["L0_attn_x"], dtype=np.float32))
    Wq = np.asarray(inputs["L0_attn_Wq"], dtype=np.float32)
    Wk = np.asarray(inputs["L0_attn_Wk"], dtype=np.float32)
    Wv = np.asarray(inputs["L0_attn_Wv"], dtype=np.float32)
    Wgk1 = np.ascontiguousarray(np.asarray(inputs["L0_attn_Wgk1"], dtype=np.float32))
    Wgk2 = np.asarray(inputs["L0_attn_Wgk2"], dtype=np.float32)
    bgk2 = np.asarray(inputs["L0_attn_bgk2"], dtype=np.float32)
    Wg = np.asarray(inputs["L0_attn_Wg"], dtype=np.float32)
    Wnorm = np.asarray(inputs["L0_attn_Wnorm"], dtype=np.float32)
    Wo = np.asarray(inputs["L0_attn_Wo"], dtype=np.float32)
    lut = np.asarray(inputs["L0_ffn_lut"], dtype=np.float32)
    gidx = np.asarray(inputs["L0_ffn_gate_idx"], dtype=np.int32)
    uidx = np.asarray(inputs["L0_ffn_up_idx"], dtype=np.int32)
    didx = np.asarray(inputs["L0_ffn_down_idx"], dtype=np.int32)

    u128 = np.ascontiguousarray(np.triu(np.ones((128, 128), np.float32))
                                * (1.0 / 16.0))
    trilm = np.ascontiguousarray(np.triu(np.ones((128, 128), np.float32)))

    maps = []
    for c in range(8):
        b, p = c // 2, c % 2
        pb = 1 - p
        m = {}
        m["x_all"] = np.ascontiguousarray(x[b])
        m["x_own"] = np.ascontiguousarray(x[b, p * TOK:(p + 1) * TOK])
        m["wq"] = np.ascontiguousarray(Wq[:, p * QF:(p + 1) * QF])
        m["wk"] = np.ascontiguousarray(Wk[:, p * QF:(p + 1) * QF])
        m["wv"] = np.ascontiguousarray(Wv[:, p * VF:(p + 1) * VF])
        m["wg"] = np.ascontiguousarray(Wg[:, p * VF:(p + 1) * VF])
        m["wgk1"] = Wgk1
        m["wgk2"] = np.ascontiguousarray(Wgk2[:, p * QF:(p + 1) * QF])
        m["bgk2v"] = np.ascontiguousarray(bgk2[p * QF:(p + 1) * QF][None, :])
        m["wnorm2"] = np.ascontiguousarray(np.tile(Wnorm, 2)[None, :])
        m["wo"] = np.ascontiguousarray(np.concatenate(
            [Wo[pb * VF:(pb + 1) * VF], Wo[p * VF:(p + 1) * VF]], axis=0))
        m["lutv"] = np.ascontiguousarray(lut[None, :])
        mv = [0.0, 1.0, 1.0, 0.0] if p == 0 else [1.0, 0.0, 0.0, 1.0]
        m["msk"] = np.ascontiguousarray(
            np.broadcast_to(np.asarray(mv, np.float32)[None, :], (128, 4)))
        m["u128"] = u128
        m["trilm"] = trilm
        m["gidxT"] = np.ascontiguousarray(gidx[:, c * SH_G:(c + 1) * SH_G].T)
        m["uidxT"] = np.ascontiguousarray(uidx[:, c * SH_G:(c + 1) * SH_G].T)
        m["didxT"] = np.ascontiguousarray(didx[c * SH_D:(c + 1) * SH_D, :].T)
        maps.append(m)
    return maps


def kernel(**inputs):
    maps = _host_inputs(inputs)
    nc = build_nc()
    res = run_bass_kernel_spmd(nc, maps, core_ids=list(range(8)))
    out = np.empty((B, L, D), np.float32)
    for c in range(8):
        b, p = c // 2, c % 2
        out[b, p * TOK:(p + 1) * TOK, :] = res.results[c]["out_own"]
    if _BUILD_DEBUG:
        kernel._last_results = res.results
    return out
